# version 01
# baseline (speedup 1.0000x reference)
import numpy as np
import concourse.bass as bass
import concourse.mybir as mybir
import bass_rust as _br
from concourse.bass_utils import run_bass_kernel_spmd
from contextlib import ExitStack

F32 = mybir.dt.float32
BF16 = mybir.dt.bfloat16
FP8 = mybir.dt.float8e4
I32 = mybir.dt.int32
AF = mybir.ActivationFunctionType
ALU = mybir.AluOpType
AXL = mybir.AxisListType

T_LEN = 2048
C = 1024
NT = 16
NCH = 8
NQB = 4
NHP = 8
LAT = 256
EPS = 1e-5
ATT_SCALE = 0.125
VT_COLS = 16 * 16 * 65


class Ctr:
    def __init__(self, h):
        self.h = h
        self.n = 0


class KB:
    ENGS = ("t", "v", "s", "g", "y")

    def __init__(self):
        self.ops = {e: [] for e in self.ENGS}
        self.tmode_cur = [None]

    def emit(self, eng, fn):
        self.ops[eng].append(fn)

    def wait(self, eng, ctr, val=None):
        v = ctr.n if val is None else val
        if v <= 0:
            return
        h = ctr.h
        self.emit(eng, lambda e: e.wait_ge(h, v))

    def op(self, eng, fn, inc=None):
        if inc is None:
            self.emit(eng, fn)
        else:
            inc.n += 1
            h = inc.h
            self.emit(eng, fn)
            self.emit(eng, lambda e: e.drain(fusable=False).then_inc(h, 1))

    def dinc(self, eng, ctr):
        ctr.n += 1
        h = ctr.h
        self.emit(eng, lambda e: e.drain(fusable=False).then_inc(h, 1))

    def tmode(self, m):
        if self.tmode_cur[0] != m:
            if self.tmode_cur[0] is not None:
                self.emit("t", lambda e: e.drain(fusable=False))
            self.tmode_cur[0] = m


def build_program(n_sublayers=8, debug=False):
    nc = bass.Bass("TRN2", target_bir_lowering=False)

    x0 = nc.dram_tensor("x0", [T_LEN, C], F32, kind="ExternalInput")
    x1 = nc.dram_tensor("x1", [T_LEN, C], F32, kind="ExternalInput")
    wqT = nc.dram_tensor("wqT", [C, C], F32, kind="ExternalInput")
    wkT = nc.dram_tensor("wkT", [C, C], F32, kind="ExternalInput")
    wvT = nc.dram_tensor("wvT", [C, C], F32, kind="ExternalInput")
    woT = nc.dram_tensor("woT", [C, C], F32, kind="ExternalInput")
    aq = nc.dram_tensor("aq", [C, C], F32, kind="ExternalInput")
    akd = nc.dram_tensor("akd", [C, 512], F32, kind="ExternalInput")
    aku = nc.dram_tensor("aku", [LAT, 2 * C], F32, kind="ExternalInput")
    awo = nc.dram_tensor("awo", [C, C], F32, kind="ExternalInput")
    fwT = nc.dram_tensor("fwT", [C, C], F32, kind="ExternalInput")
    bq_h = nc.dram_tensor("bq", [128, 8], F32, kind="ExternalInput")
    bkd_h = nc.dram_tensor("bkd", [128, 4], F32, kind="ExternalInput")
    bkuk_h = nc.dram_tensor("bkuk", [128, 8], F32, kind="ExternalInput")
    bkuv_h = nc.dram_tensor("bkuv", [128, 1024], F32, kind="ExternalInput")
    bo_h = nc.dram_tensor("bo", [128, C], F32, kind="ExternalInput")
    if debug:
        dbg_st = nc.dram_tensor("dbg_st", [128, 128], F32,
                                kind="ExternalOutput")
        dbg_xtm = nc.dram_tensor("dbg_xtm", [128, NT * C], BF16,
                                 kind="ExternalOutput")
        dbg_xn = nc.dram_tensor("dbg_xn", [128, NT * C], BF16,
                                kind="ExternalOutput")
        dbg_q = nc.dram_tensor("dbg_q", [128, NT * C], BF16,
                               kind="ExternalOutput")
        dbg_k = nc.dram_tensor("dbg_k", [128, NT * C], BF16,
                               kind="ExternalOutput")
        dbg_vt = nc.dram_tensor("dbg_vt", [128, VT_COLS], BF16,
                                kind="ExternalOutput")
    y0 = nc.dram_tensor("y0", [T_LEN, C], F32, kind="ExternalOutput")
    y1 = nc.dram_tensor("y1", [T_LEN, C], F32, kind="ExternalOutput")
    wt8 = nc.dram_tensor("wt8", [C, C], FP8)

    with ExitStack() as ctx:
        en = ctx.enter_context
        DW = Ctr(en(nc.semaphore("DW")))
        DY = Ctr(en(nc.semaphore("DY")))
        DO = Ctr(en(nc.semaphore("DO")))
        ST = Ctr(en(nc.semaphore("ST")))
        SV = Ctr(en(nc.semaphore("SV")))
        SS = Ctr(en(nc.semaphore("SS")))
        SG = Ctr(en(nc.semaphore("SG")))

        res = en(nc.sbuf_tensor("res", [128, NT * C], F32))
        B1 = en(nc.sbuf_tensor("B1", [128, NT * C], BF16))
        B2 = en(nc.sbuf_tensor("B2", [128, NT * C], BF16))
        KF = en(nc.sbuf_tensor("KF", [128, NT * C], BF16))
        VT = en(nc.sbuf_tensor("VT", [128, VT_COLS], BF16))
        WS = en(nc.sbuf_tensor("WS", [128, 2 * 1024], BF16))
        idb = en(nc.sbuf_tensor("idb", [128, 128], BF16))
        maskb = en(nc.sbuf_tensor("maskb", [128, 128], BF16))
        onesr = en(nc.sbuf_tensor("onesr", [128, 128], BF16))
        stats = en(nc.sbuf_tensor("stats", [128, 128], F32))
        bo_bf = en(nc.sbuf_tensor("bo_bf", [128, 1024], BF16))
        foldt = en(nc.sbuf_tensor("foldt", [128, 1024], BF16))
        c128 = en(nc.sbuf_tensor("c128", [128, 128], BF16))
        bq_t = en(nc.sbuf_tensor("bq_t", [128, 8], F32))
        bkd_t = en(nc.sbuf_tensor("bkd_t", [128, 4], F32))
        bkuk_t = en(nc.sbuf_tensor("bkuk_t", [128, 8], F32))
        onesf = en(nc.sbuf_tensor("onesf", [128, 128], F32))

        PS0 = en(nc.psum_tensor("PS0", [128, 1024], F32))
        PS1 = en(nc.psum_tensor("PS1", [128, 1024], F32))
        PA0 = en(nc.psum_tensor("PA0", [128, 512], F32))
        PA1 = en(nc.psum_tensor("PA1", [128, 512], F32))
        PT0 = en(nc.psum_tensor("PT0", [128, 1024], BF16))
        PT1 = en(nc.psum_tensor("PT1", [128, 1024], BF16))

        s_sumx = stats[:, 0:16]
        s_sumxx = stats[:, 16:32]
        s_m = stats[:, 32:48]
        s_var = stats[:, 48:64]
        s_r = stats[:, 64:80]
        s_nmr = stats[:, 80:96]
        s_eps = stats[:, 96:97]
        s_scale = stats[:, 97:98]
        s_rscale = stats[:, 98:99]
        s_tot = stats[:, 99:100]
        s_rden = stats[:, 100:108]
        s_msq = stats[:, 108:124]
        s_zero = stats[:, 124:125]

        B1f = B1.bitcast(F32)
        KFf = KF.bitcast(F32)
        VTf = VT.bitcast(F32)
        VT8 = VT.bitcast(FP8)
        B1i = B1.bitcast(I32)

        pb = [B2[:, 0:1024], B2[:, 1024:2048]]
        avfm = B2[:, 2048:6144]
        avn = [B2[:, 6144:6400], B2[:, 6400:6656]]
        wo_res = B2[:, 6656:14848]
        bias_row = B2[:, 14848:15872]
        scr1 = B2[:, 0:1024]
        scr2 = B2[:, 1024:2048]

        kb = KB()
        res3 = bass.AP(res, 0, [[NT * C, 128], [C, NT], [1, C]])

        def dma_g(dst_ap, src_ap, guard=None):
            if guard is not None:
                kb.wait("g", guard)
            DW.n += 16
            kb.emit("g", lambda e: e.dma_start(dst_ap, src_ap).then_inc(DW.h, 16))

        def dma_y(dst_ap, src_ap, guard=None):
            if guard is not None:
                kb.wait("y", guard)
            DY.n += 16
            kb.emit("y", lambda e: e.dma_start(dst_ap, src_ap).then_inc(DY.h, 16))

        def col_slices_hbm(w2d, co, rows, base_col=0):
            a = base_col + co * 128
            return w2d[:, a:a + 128].rearrange("(ci p) co -> p ci co", p=128)

        def whole_hbm(w2d):
            return w2d.rearrange("(ci p) co -> p ci co", p=128)

        gen_banks = [PS0[:, 0:512], PS0[:, 512:1024],
                     PS1[:, 0:512], PS1[:, 512:1024]]
        gb = {"i": 0, "free": [(0, None)] * 4}

        def next_bank():
            i = gb["i"] % 4
            gb["i"] += 1
            v, ctr = gb["free"][i]
            if ctr is not None:
                kb.wait("t", ctr, v)
            return i, gen_banks[i]

        def free_bank(i, ctr):
            gb["free"][i] = (ctr.n, ctr)

        kb.op("g", lambda e: e.iota(B1i[:, 0:128], [[-1, 128]], base=0,
                                    channel_multiplier=1), inc=SG)
        kb.wait("v", SG)
        kb.op("v", lambda e: e.tensor_scalar(idb[:], B1i[:, 0:128], 0.0, None,
                                             ALU.is_equal))
        kb.op("v", lambda e: e.tensor_scalar(maskb[:], B1i[:, 0:128], 0.0, None,
                                             ALU.is_le))
        kb.op("v", lambda e: e.memset(onesr[0:1, :], 1.0))
        kb.op("v", lambda e: e.memset(s_eps, EPS))
        kb.op("v", lambda e: e.memset(s_zero, 0.0))
        kb.op("v", lambda e: e.memset(onesf[:], 1.0))
        kb.op("v", lambda e: e.memset(c128[:], 1.0 / 128.0))
        vt_ones = bass.AP(VT, 64, [[VT_COLS, 128], [65, 256]])
        kb.op("v", lambda e: e.memset(vt_ones, 1.0))
        kb.dinc("v", SV)
        dma_y(bq_t[:], bq_h[:])
        dma_y(bkd_t[:], bkd_h[:])
        dma_y(bkuk_t[:], bkuk_h[:])
        dma_g(foldt[:], bkuv_h[:])
        dma_g(bo_bf[:], bo_h[:])
        dma_y(res3, x0.rearrange("(t p) c -> p t c", p=128))
        setup_dy = DY.n
        setup_dw = DW.n

        def ln_and_transpose(first):
            kb.wait("s", ST)
            kb.wait("v", ST)
            if first:
                kb.wait("v", DY, setup_dy)
                kb.wait("s", DY, setup_dy)
                kb.wait("t", DW, setup_dw)
            kb.wait("s", SV)
            for t in range(NT):
                rt = res[:, t * C:(t + 1) * C]
                kb.op("v", lambda e, rt=rt, t=t: e.tensor_scalar(
                    scr1[:], rt, 1.0, None, ALU.mult, op1=ALU.add,
                    accum_out=s_sumx[:, t:t + 1]))
                kb.op("s", lambda e, rt=rt, t=t: e.activation(
                    scr2[:], rt, AF.Square, accum_out=s_sumxx[:, t:t + 1]))
            kb.dinc("v", SV)
            kb.dinc("s", SS)
            kb.wait("v", SS)
            kb.op("v", lambda e: e.tensor_scalar(s_m, s_sumx, 1.0 / C, None,
                                                 ALU.mult))
            kb.dinc("v", SV)
            kb.op("v", lambda e: e.tensor_tensor(s_msq, s_m, s_m, ALU.mult))
            kb.dinc("v", SV)
            kb.op("v", lambda e: e.scalar_tensor_tensor(
                s_var, s_sumxx, 1.0 / C, s_msq, ALU.mult, ALU.subtract))
            kb.dinc("v", SV)
            kb.wait("s", SV)
            kb.op("s", lambda e: e.activation(s_r, s_var, AF.Ln,
                                              bias=s_eps, scale=1.0))
            kb.dinc("s", SS)
            kb.op("s", lambda e: e.activation(s_r, s_r, AF.Exp, scale=-0.5))
            kb.dinc("s", SS)
            kb.wait("v", SS)
            kb.op("v", lambda e: e.scalar_tensor_tensor(
                s_nmr, s_m, -1.0, s_r, ALU.mult, ALU.mult))
            kb.dinc("v", SV)
            for t in range(NT):
                rt = res[:, t * C:(t + 1) * C]
                xt = B1[:, t * C:(t + 1) * C]
                kb.op("v", lambda e, rt=rt, xt=xt, t=t: e.tensor_scalar(
                    xt, rt, s_r[:, t:t + 1], s_nmr[:, t:t + 1],
                    ALU.mult, ALU.add))
            kb.dinc("v", SV)
            kb.wait("t", SV)
            kb.tmode("trans")
            for t in range(NT):
                PT = PT0 if t % 2 == 0 else PT1
                if t >= 2:
                    kb.wait("t", SV)
                for ch in range(NCH):
                    src = B1[:, t * C + ch * 128: t * C + (ch + 1) * 128]
                    dst = PT[:, ch * 128:(ch + 1) * 128]
                    kb.op("t", lambda e, d=dst, s_=src: e.transpose(
                        d, s_, idb[:]), inc=(ST if ch == NCH - 1 else None))
                kb.wait("v", ST)
                src = bass.AP(PT, 0, [[1024, 128], [128, NCH], [1, 128]])
                dst = bass.AP(B2, t * 128, [[NT * C, 128], [T_LEN, NCH],
                                            [1, 128]])
                kb.op("v", lambda e, d=dst, s_=src: e.tensor_copy(d, s_))
                kb.dinc("v", SV)
            kb.wait("t", SV)

        def proj_wstat(w_hbm, n_ci, n_co, rhs_col0, evac_fn, w_rows,
                       base_col=0):
            slot = [WS[:, 0:1024], WS[:, 1024:2048]]
            slot_rows = w_rows // 128
            mm_done = {}
            dw_at = {}
            def slot3(i):
                return bass.AP(WS, i * 1024,
                               [[2048, 128], [128, slot_rows], [1, 128]])

            for co in range(min(2, n_co)):
                dma_g(slot3(co % 2), col_slices_hbm(w_hbm, co, w_rows,
                                                    base_col), guard=ST)
                dw_at[co] = DW.n
            kb.tmode("full")
            for co in range(n_co):
                kb.wait("t", DW, dw_at[co])
                sl = slot[co % 2]
                for qb in range(NQB):
                    bi, bank = next_bank()
                    for ci in range(n_ci):
                        lhsT = sl[:, ci * 128:(ci + 1) * 128]
                        rhs = rhs_col0(ci, qb)
                        kb.op("t", lambda e, b=bank, l=lhsT, r=rhs,
                              st=(ci == 0), sp=(ci == n_ci - 1): e.matmul(
                                  b, l, r, start=st, stop=sp,
                                  skip_group_check=True),
                              inc=(ST if ci == n_ci - 1 else None))
                    evac_fn(co, qb, bank, bi)
                mm_done[co] = ST.n
                nxt = co + 2
                if nxt < n_co:
                    kb.wait("g", ST, mm_done[co])
                    dma_g(slot3(nxt % 2),
                          col_slices_hbm(w_hbm, nxt, w_rows, base_col))
                    dw_at[nxt] = DW.n

        def evac_to_fm(dst_tensor, dst_base, eng, sem, bias_col=None,
                       func=AF.Identity):
            def f(co, qb, bank, bi):
                dst = dst_tensor[:, dst_base + co * T_LEN + qb * 512:
                                 dst_base + co * T_LEN + qb * 512 + 512]
                kb.wait(eng, ST)
                if eng == "s":
                    b = (bias_col[:, co:co + 1] if bias_col is not None
                         else s_zero)
                    kb.op("s", lambda e, d=dst, p=bank, b=b: e.activation(
                        d, p, func, bias=b, scale=1.0), inc=sem)
                else:
                    kb.op("v", lambda e, d=dst, p=bank: e.tensor_copy(d, p),
                          inc=sem)
                free_bank(bi, sem)
            return f

        def proj_v(lhs_col0, n_ci, rhs_res):
            kb.tmode("full")
            for qt in range(NT):
                for eh in range(2):
                    bi, bank = next_bank()
                    for ci in range(n_ci):
                        lhsT = lhs_col0(ci, qt)
                        rhs = rhs_res[:, ci * 1024 + eh * 512:
                                      ci * 1024 + eh * 512 + 512]
                        kb.op("t", lambda e, b=bank, l=lhsT, r=rhs,
                              st=(ci == 0), sp=(ci == n_ci - 1): e.matmul(
                                  b, l, r, start=st, stop=sp,
                                  skip_group_check=True),
                              inc=(ST if ci == n_ci - 1 else None))
                    kb.wait("v", ST)
                    dst = bass.AP(VT, (qt * 16 + eh * 8) * 65,
                                  [[VT_COLS, 128], [65, 8], [1, 64]])
                    src = bass.AP(bank.tensor, bank.offset,
                                  [[bank.tensor.shape[1], 128], [64, 8],
                                   [1, 64]])
                    kb.op("v", lambda e, d=dst, s_=src: e.tensor_copy(d, s_),
                          inc=SV)
                    free_bank(bi, SV)

        def attention(q_t, k_t, with_bias_row):
            kc_par = {"i": 0}
            ps_exp = [0, 0]
            ps_sv = [0, 0]
            av_free_sv = SV.n

            for qb in range(NQB):
                n_kc = 4 * (qb + 1)
                for hp in range(NHP):
                    for kc in range(n_kc):
                        buf = kc_par["i"] % 2
                        kc_par["i"] += 1
                        PS = PS0 if buf == 0 else PS1
                        diag = kc >= 4 * qb
                        off = (kc - 4 * qb) * 128 if diag else 0
                        w = 512 - off
                        kb.tmode("row64")
                        kb.wait("t", SS, ps_exp[buf])
                        if ps_sv[buf]:
                            kb.wait("t", SV, ps_sv[buf])
                            ps_sv[buf] = 0
                        for h2 in range(2):
                            rb = h2 * 64
                            kcol = hp * T_LEN + kc * 128
                            qcol = hp * T_LEN + qb * 512 + off
                            lhsT = k_t[rb:rb + 64, kcol:kcol + 128]
                            rhs = q_t[rb:rb + 64, qcol:qcol + w]
                            out = PS[:, h2 * 512 + off: h2 * 512 + off + w]
                            kb.op("t", lambda e, o=out, l=lhsT, r=rhs, rb=rb:
                                  e.matmul(o, l, r, start=True, stop=True,
                                           skip_group_check=True,
                                           tile_position=(rb, 0)),
                                  inc=(ST if h2 == 1 else None))
                        kb.wait("s", ST)
                        src = bass.AP(PS, off, [[1024, 128], [512, 2], [1, w]])
                        dst = bass.AP(pb[buf].tensor, pb[buf].offset + off,
                                      [[NT * C, 128], [512, 2], [1, w]])
                        kb.op("s", lambda e, d=dst, s_=src: e.activation(
                            d, s_, AF.Exp, scale=ATT_SCALE))
                        kb.dinc("s", SS)
                        ps_exp[buf] = SS.n
                        if diag:
                            kb.wait("v", SS)
                            mt = bass.AP(pb[buf].tensor, pb[buf].offset + off,
                                         [[NT * C, 128], [512, 2], [1, 128]])
                            mb = bass.AP(maskb, 0,
                                         [[128, 128], [0, 2], [1, 128]])
                            kb.op("v", lambda e, m=mt, mb=mb: e.tensor_tensor(
                                m, m, mb, ALU.mult))
                            kb.dinc("v", SV)
                            mask_sv = SV.n
                        if diag:
                            kb.wait("t", SV, mask_sv)
                        else:
                            kb.wait("t", SS)
                        if kc == 0 and av_free_sv:
                            kb.wait("t", SV, av_free_sv)
                        kb.tmode("full")
                        qt_lo = max(0, kc - 4 * qb)
                        for h2 in range(2):
                            h = 2 * hp + h2
                            bank = PA0 if h2 == 0 else PA1
                            for qtl in range(qt_lo, 4):
                                lhsT = pb[buf][:, h2 * 512 + qtl * 128:
                                               h2 * 512 + (qtl + 1) * 128]
                                rhs = bass.AP(VT, (kc * 16 + h) * 65,
                                              [[VT_COLS, 128], [1, 65]])
                                out = bank[:, qtl * 65: qtl * 65 + 65]
                                first = (kc == 0 and qtl == 0)
                                last = (kc == n_kc - 1 and qtl == 3 and
                                        h2 == 1)
                                kb.op("t", lambda e, o=out, l=lhsT, r=rhs,
                                      st=first: e.matmul(
                                          o, l, r, start=st, stop=False,
                                          skip_group_check=True),
                                      inc=(ST if last else None))
                    kb.wait("v", ST)
                    for h2 in range(2):
                        bank = PA0 if h2 == 0 else PA1
                        for qtl in range(4):
                            dcol = bank[:, qtl * 65 + 64: qtl * 65 + 65]
                            ix = h2 * 4 + qtl
                            kb.op("v", lambda e, d=dcol, ix=ix: e.reciprocal(
                                s_rden[:, ix:ix + 1], d))
                    kb.dinc("v", SV)
                    for h2 in range(2):
                        bank = PA0 if h2 == 0 else PA1
                        for qtl in range(4):
                            ix = h2 * 4 + qtl
                            src = bank[:, qtl * 65: qtl * 65 + 64]
                            dst = avn[h2][:, qtl * 64:(qtl + 1) * 64]
                            kb.op("v", lambda e, d=dst, s_=src, ix=ix:
                                  e.tensor_scalar(d, s_,
                                                  s_rden[:, ix:ix + 1],
                                                  None, ALU.mult))
                    kb.dinc("v", SV)
                    av_free_sv = SV.n
                    PT = PT0 if hp % 2 == 0 else PT1
                    kb.wait("t", SV)
                    kb.tmode("trans")
                    for h2 in range(2):
                        for qtl in range(4):
                            src = avn[h2][:, qtl * 64:(qtl + 1) * 64]
                            dst = PT[h2 * 64:h2 * 64 + 64,
                                     qtl * 128:(qtl + 1) * 128]
                            last = (h2 == 1 and qtl == 3)
                            kb.op("t", lambda e, d=dst, s_=src, h2=h2:
                                  e.transpose(d, s_, idb[:],
                                              tile_position=(0, h2 * 64)),
                                  inc=(ST if last else None))
                    kb.wait("v", ST)
                    dst = avfm[:, hp * 512:(hp + 1) * 512]
                    kb.op("v", lambda e, d=dst, s_=PT: e.tensor_copy(
                        d, s_[:, 0:512]))
                    kb.dinc("v", SV)
                kb.wait("t", SV)
                kb.tmode("full")
                for qtl in range(4):
                    qt_g = qb * 4 + qtl
                    for ob in range(2):
                        bi, bank = next_bank()
                        for ch in range(NCH):
                            lhsT = avfm[:, ch * 512 + qtl * 128:
                                        ch * 512 + (qtl + 1) * 128]
                            rhs = wo_res[:, ch * 1024 + ob * 512:
                                         ch * 1024 + ob * 512 + 512]
                            last_mm = (ch == NCH - 1) and not with_bias_row
                            kb.op("t", lambda e, b=bank, l=lhsT, r=rhs,
                                  st=(ch == 0), sp=last_mm: e.matmul(
                                      b, l, r, start=st, stop=sp,
                                      skip_group_check=True),
                                  inc=(ST if last_mm else None))
                        if with_bias_row:
                            rhs = bias_row[:, ob * 512:(ob + 1) * 512]
                            kb.op("t", lambda e, b=bank, l=c128[:], r=rhs:
                                  e.matmul(b, l, r, start=False, stop=True,
                                           skip_group_check=True), inc=ST)
                        kb.wait("v", ST)
                        dstr = res[:, qt_g * C + ob * 512:
                                   qt_g * C + ob * 512 + 512]
                        kb.op("v", lambda e, d=dstr, p=bank: e.tensor_tensor(
                            d, d, p, ALU.add), inc=SV)
                        free_bank(bi, SV)
                ps_sv[0] = SV.n
                ps_sv[1] = SV.n
            kb.dinc("v", SV)

        def attn_a():
            ln_and_transpose(first=(sub_idx[0] == 0))
            if debug:
                kb.wait("y", SV)
                dma_y(dbg_st[:], stats[:])
                dma_y(dbg_xtm[:], B1[:])

            def xnT_rhs(ci, qb):
                return B2[:, ci * T_LEN + qb * 512: ci * T_LEN + qb * 512 + 512]

            def xnT_lhs(ci, qt):
                return B2[:, ci * T_LEN + qt * 128: ci * T_LEN + qt * 128 + 128]

            kb.wait("v", ST)
            kb.op("v", lambda e: e.memset(vt_ones, 1.0))
            kb.dinc("v", SV)
            proj_wstat(wkT, NCH, NCH, xnT_rhs,
                       evac_to_fm(KF, 0, "s", SS), C)
            kb.dinc("s", SS)
            dma_g(B1.reshape([128, 16, 1024])[:, 0:8, :], whole_hbm(wvT),
                  guard=ST)
            wv_dw = DW.n
            kb.wait("t", DW, wv_dw)
            proj_v(xnT_lhs, NCH, B1[:, 0:8192])
            proj_wstat(wqT, NCH, NCH, xnT_rhs,
                       evac_to_fm(B1, 0, "v", SV), C)
            kb.dinc("v", SV)
            dst = bass.AP(B2, 6656, [[NT * C, 128], [1024, 8], [1, 1024]])
            dma_g(dst, whole_hbm(woT), guard=ST)
            kb.wait("t", DW)
            kb.wait("t", SV)
            kb.wait("t", SS)
            if debug:
                kb.wait("y", SV)
                kb.wait("y", SS)
                dma_y(dbg_xn[:], B2[:])
                dma_y(dbg_q[:], B1[:])
                dma_y(dbg_k[:], KF[:])
                dma_y(dbg_vt[:], VT[:])
            attention(B1, KF, with_bias_row=False)

        def ffn():
            ln_and_transpose(first=False)

            def xnT_lhs(ci, qt):
                return B2[:, ci * T_LEN + qt * 128: ci * T_LEN + qt * 128 + 128]

            if not ffn_state["q"]:
                dma_y(bass.AP(KFf, 0, [[8192, 128], [1024, 8], [1, 1024]]),
                      whole_hbm(fwT), guard=ST)
                kb.wait("v", DY)
                kb.wait("v", ST)
                w_all = KFf[:, 0:8192]
                kb.op("v", lambda e: e.tensor_reduce(
                    s_tot, w_all, AXL.X, ALU.add, apply_absolute_value=True))
                kb.dinc("v", SV)
                kb.wait("t", SV)
                kb.tmode("fp32a")
                kb.op("t", lambda e: e.matmul(
                    PA0[0:1, 0:1], onesf[:, 0:1], s_tot, start=True,
                    stop=True, skip_group_check=True), inc=ST)
                kb.wait("v", ST)
                kb.op("v", lambda e: e.tensor_copy(s_msq[0:1, 0:1],
                                                   PA0[0:1, 0:1]))
                kb.dinc("v", SV)
                kb.wait("t", SV)
                kb.tmode("fp32b")
                kb.op("t", lambda e: e.matmul(
                    PA1[:, 0:1], onesf[0:1, :], s_msq[0:1, 0:1], start=True,
                    stop=True, skip_group_check=True), inc=ST)
                kb.wait("v", ST)
                kb.op("v", lambda e: e.tensor_scalar(
                    s_scale, PA1[:, 0:1], 1.0 / (C * C), 1e-5,
                    ALU.mult, ALU.max))
                kb.dinc("v", SV)
                kb.op("v", lambda e: e.reciprocal(s_rscale, s_scale))
                kb.dinc("v", SV)
                u = VTf[:, 0:8192]
                kb.op("v", lambda e: e.tensor_scalar(
                    u, w_all, s_rscale[:, 0:1], None, ALU.mult))
                kb.dinc("v", SV)
                g_ = B1f[:, 0:8192]
                l_ = KFf[:, 0:8192]
                kb.op("v", lambda e: e.tensor_scalar(g_, u, 0.5, None,
                                                     ALU.is_gt))
                kb.op("v", lambda e: e.tensor_scalar(l_, u, -0.5, None,
                                                     ALU.is_lt))
                kb.dinc("v", SV)
                kb.op("v", lambda e: e.tensor_tensor(
                    VT8[:, 0:8192], g_, l_, ALU.subtract))
                kb.dinc("v", SV)
                kb.wait("y", SV)
                dma_y(whole_hbm(wt8),
                      bass.AP(VT8, 0, [[2 * VT_COLS, 128], [1024, 8],
                                       [1, 1024]]))
                ffn_state["q"] = True
                ffn_state["store_dy"] = DY.n
            else:
                dma_y(bass.AP(VT8, 0, [[2 * VT_COLS, 128], [1024, 8],
                                       [1, 1024]]),
                      whole_hbm(wt8), guard=ST)
            wt_dy = DY.n
            kb.wait("t", DY, wt_dy)
            for qt in range(NT):
                for ob in range(2):
                    bi, bank = next_bank()
                    for ci in range(NCH):
                        lhsT = xnT_lhs(ci, qt)
                        rhs = VT8[:, ci * 1024 + ob * 512:
                                  ci * 1024 + ob * 512 + 512]
                        kb.op("t", lambda e, b=bank, l=lhsT, r=rhs,
                              st=(ci == 0), sp=(ci == NCH - 1): e.matmul(
                                  b, l, r, start=st, stop=sp,
                                  skip_group_check=True),
                              inc=(ST if ci == NCH - 1 else None))
                    kb.wait("v", ST)
                    dstr = res[:, qt * C + ob * 512: qt * C + ob * 512 + 512]
                    kb.op("v", lambda e, d=dstr, p=bank:
                          e.scalar_tensor_tensor(d, p, s_scale[:, 0:1], d,
                                                 ALU.mult, ALU.add), inc=SV)
                    free_bank(bi, SV)
            kb.dinc("v", SV)
            if ffn_state.get("store_dy"):
                kb.wait("v", DY, ffn_state["store_dy"])
                ffn_state["store_dy"] = 0

        def attn_b():
            ln_and_transpose(first=False)

            def xnT_rhs(ci, qb):
                return B2[:, ci * T_LEN + qb * 512: ci * T_LEN + qb * 512 + 512]

            kb.wait("v", ST)
            kb.op("v", lambda e: e.memset(vt_ones, 1.0))
            kb.dinc("v", SV)
            proj_wstat(akd, NCH, 4, xnT_rhs,
                       evac_to_fm(B1, 0, "s", SS, bias_col=bkd_t,
                                  func=AF.Gelu), C)
            kb.dinc("s", SS)
            kb.wait("v", SS)
            for lc in range(2):
                dst = B1[:, (4 + lc) * T_LEN:(5 + lc) * T_LEN]
                a_ = B1[:, lc * T_LEN:(lc + 1) * T_LEN]
                b_ = B1[:, (2 + lc) * T_LEN:(3 + lc) * T_LEN]
                kb.op("v", lambda e, d=dst, a=a_, b=b_: e.tensor_tensor(
                    d, a, b, ALU.add))
            kb.dinc("v", SV)

            def lat_rhs(ci, qb):
                base = (4 + ci) * T_LEN + qb * 512
                return B1[:, base:base + 512]

            def lat_lhs(ci, qt):
                base = (4 + ci) * T_LEN + qt * 128
                return B1[:, base:base + 128]

            kb.wait("t", SV)
            proj_wstat(aku, 2, NCH, lat_rhs,
                       evac_to_fm(KF, 0, "s", SS, bias_col=bkuk_t,
                                  func=AF.Identity), LAT, base_col=0)
            kb.dinc("s", SS)
            dma_g(B1.reshape([128, 16, 1024])[:, 0:2, :],
                  aku[:, C:2 * C].rearrange("(ci p) co -> p ci co", p=128),
                  guard=SV)
            kb.wait("t", DW)
            kb.wait("t", SS)
            proj_v(lat_lhs, 2, B1[:, 0:2048])
            proj_wstat(aq, NCH, NCH, xnT_rhs,
                       evac_to_fm(B1, 0, "s", SS, bias_col=bq_t,
                                  func=AF.Identity), C)
            kb.dinc("s", SS)
            dst = bass.AP(B2, 6656, [[NT * C, 128], [1024, 8], [1, 1024]])
            dma_g(dst, whole_hbm(awo), guard=ST)
            kb.wait("t", DW)
            kb.tmode("full")
            for ob in range(2):
                bank = PA0[:, 0:512] if ob == 0 else PA1[:, 0:512]
                for ci in range(NCH):
                    lhsT = foldt[:, ci * 128:(ci + 1) * 128]
                    rhs = wo_res[:, ci * 1024 + ob * 512:
                                 ci * 1024 + ob * 512 + 512]
                    kb.op("t", lambda e, b=bank, l=lhsT, r=rhs, st=(ci == 0),
                          sp=(ci == NCH - 1): e.matmul(
                              b, l, r, start=st, stop=sp,
                              skip_group_check=True),
                          inc=(ST if ci == NCH - 1 else None))
                kb.wait("v", ST)
                dst2 = bias_row[:, ob * 512:(ob + 1) * 512]
                kb.op("v", lambda e, d=dst2, p=bank, ob=ob: e.tensor_tensor(
                    d, p, bo_bf[:, ob * 512:(ob + 1) * 512], ALU.add))
            kb.dinc("v", SV)
            kb.wait("t", SV)
            kb.wait("t", SS)
            attention(B1, KF, with_bias_row=True)

        ffn_state = {"q": False, "store_dy": 0}
        sub_idx = [0]
        phases = [attn_a, ffn, attn_b, ffn]
        for seq in range(2):
            if sub_idx[0] >= n_sublayers:
                break
            if seq == 1:
                kb.wait("y", SV)
                DO.n += 16
                kb.emit("y", lambda e: e.dma_start(
                    y0.rearrange("(t p) c -> p t c", p=128),
                    res3).then_inc(DO.h, 16))
                kb.wait("y", DO)
                dma_y(res3,
                      x1.rearrange("(t p) c -> p t c", p=128))
                xdy = DY.n
                kb.wait("v", DY, xdy)
                kb.wait("s", DY, xdy)
            for ph in phases:
                if sub_idx[0] >= n_sublayers:
                    break
                ph()
                sub_idx[0] += 1
        ran_seq1 = sub_idx[0] > 4
        kb.wait("y", SV)
        if ran_seq1:
            DO.n += 16
            kb.emit("y", lambda e: e.dma_start(
                y1.rearrange("(t p) c -> p t c", p=128),
                res3).then_inc(DO.h, 16))
        else:
            DO.n += 16
            kb.emit("y", lambda e: e.dma_start(
                y0.rearrange("(t p) c -> p t c", p=128),
                res3).then_inc(DO.h, 16))
            DO.n += 16
            kb.emit("y", lambda e: e.dma_start(
                y1.rearrange("(t p) c -> p t c", p=128),
                res3).then_inc(DO.h, 16))
        kb.wait("y", DO)

        with nc.Block() as block:
            @block.tensor
            def _(t):
                for f in kb.ops["t"]:
                    f(t)

            @block.vector
            def _(v):
                for f in kb.ops["v"]:
                    f(v)

            @block.scalar
            def _(s):
                for f in kb.ops["s"]:
                    f(s)

            @block.gpsimd
            def _(g):
                for f in kb.ops["g"]:
                    f(g)

            @block.sync
            def _(y):
                for f in kb.ops["y"]:
                    f(y)

    return nc


def _shared_inputs(inputs):
    d = inputs

    def a(x):
        return np.ascontiguousarray(np.asarray(x, dtype=np.float32))

    return {
        "wqT": a(np.asarray(d["L0_attn_a_wq"]).T),
        "wkT": a(np.asarray(d["L0_attn_a_wk"]).T),
        "wvT": a(np.asarray(d["L0_attn_a_wv"]).T),
        "woT": a(np.asarray(d["L0_attn_a_wo"]).T),
        "aq": a(d["L0_attn_b_Wq"]),
        "akd": a(d["L0_attn_b_Wkd"]),
        "aku": a(d["L0_attn_b_Wku"]),
        "awo": a(d["L0_attn_b_Wo"]),
        "fwT": a(np.asarray(d["L0_ffn_weight"]).T),
        "bq": a(np.asarray(d["L0_attn_b_bq"]).reshape(8, 128).T),
        "bkd": a(np.asarray(d["L0_attn_b_bkd"]).reshape(4, 128).T),
        "bkuk": a(np.asarray(d["L0_attn_b_bku"])[:1024].reshape(8, 128).T),
        "bkuv": a(np.broadcast_to(
            np.asarray(d["L0_attn_b_bku"])[1024:].reshape(8, 128).T[:, :, None],
            (128, 8, 128)).reshape(128, 1024)),
        "bo": a(np.broadcast_to(
            np.asarray(d["L0_attn_b_bo"])[None, :], (128, 1024))),
    }


def kernel(**inputs):
    nc = build_program()
    sh = _shared_inputs(inputs)
    x = np.asarray(inputs["L0_attn_a_x"], dtype=np.float32)
    in_maps = []
    for core in range(8):
        m = dict(sh)
        m["x0"] = np.ascontiguousarray(x[2 * core])
        m["x1"] = np.ascontiguousarray(x[2 * core + 1])
        in_maps.append(m)
    res = run_bass_kernel_spmd(nc, in_maps, core_ids=list(range(8)))
    out = np.empty((16, T_LEN, C), dtype=np.float32)
    for core in range(8):
        out[2 * core] = res.results[core]["y0"]
        out[2 * core + 1] = res.results[core]["y1"]
    return out


# version 02
# speedup vs baseline: 1.0852x; 1.0852x over previous
import numpy as np
import concourse.bass as bass
import concourse.mybir as mybir
import bass_rust as _br
from concourse.bass_utils import run_bass_kernel_spmd
from contextlib import ExitStack

F32 = mybir.dt.float32
BF16 = mybir.dt.bfloat16
FP8 = mybir.dt.float8e4
I32 = mybir.dt.int32
AF = mybir.ActivationFunctionType
ALU = mybir.AluOpType
AXL = mybir.AxisListType

T_LEN = 2048
C = 1024
NT = 16
NCH = 8
NQB = 4
NHP = 8
LAT = 256
EPS = 1e-5
ATT_SCALE = 0.125
VT_COLS = 16 * 16 * 65


class Ctr:
    def __init__(self, h):
        self.h = h
        self.n = 0


class KB:
    ENGS = ("t", "v", "s", "g", "y")

    def __init__(self):
        self.ops = {e: [] for e in self.ENGS}
        self.tmode_cur = [None]

    def emit(self, eng, fn):
        self.ops[eng].append(fn)

    def wait(self, eng, ctr, val=None):
        v = ctr.n if val is None else val
        if v <= 0:
            return
        h = ctr.h
        self.emit(eng, lambda e: e.wait_ge(h, v))

    def op(self, eng, fn, inc=None):
        if inc is None:
            self.emit(eng, fn)
        else:
            inc.n += 1
            h = inc.h
            self.emit(eng, fn)
            self.emit(eng, lambda e: e.drain(fusable=False).then_inc(h, 1))

    def dinc(self, eng, ctr):
        ctr.n += 1
        h = ctr.h
        self.emit(eng, lambda e: e.drain(fusable=False).then_inc(h, 1))

    TMODE_DRAINS = False
    T_DRAIN_INCS = False

    def tmode(self, m):
        if self.tmode_cur[0] != m:
            if self.TMODE_DRAINS and self.tmode_cur[0] is not None:
                self.emit("t", lambda e: e.drain(fusable=False))
            self.tmode_cur[0] = m


def build_program(n_sublayers=8, debug=False):
    nc = bass.Bass("TRN2", target_bir_lowering=False)

    x0 = nc.dram_tensor("x0", [T_LEN, C], F32, kind="ExternalInput")
    x1 = nc.dram_tensor("x1", [T_LEN, C], F32, kind="ExternalInput")
    wqT = nc.dram_tensor("wqT", [C, C], F32, kind="ExternalInput")
    wkT = nc.dram_tensor("wkT", [C, C], F32, kind="ExternalInput")
    wvT = nc.dram_tensor("wvT", [C, C], F32, kind="ExternalInput")
    woT = nc.dram_tensor("woT", [C, C], F32, kind="ExternalInput")
    aq = nc.dram_tensor("aq", [C, C], F32, kind="ExternalInput")
    akd = nc.dram_tensor("akd", [C, 512], F32, kind="ExternalInput")
    aku = nc.dram_tensor("aku", [LAT, 2 * C], F32, kind="ExternalInput")
    awo = nc.dram_tensor("awo", [C, C], F32, kind="ExternalInput")
    fwT = nc.dram_tensor("fwT", [C, C], F32, kind="ExternalInput")
    bq_h = nc.dram_tensor("bq", [128, 8], F32, kind="ExternalInput")
    bkd_h = nc.dram_tensor("bkd", [128, 4], F32, kind="ExternalInput")
    bkuk_h = nc.dram_tensor("bkuk", [128, 8], F32, kind="ExternalInput")
    bkuv_h = nc.dram_tensor("bkuv", [128, 1024], F32, kind="ExternalInput")
    bo_h = nc.dram_tensor("bo", [128, C], F32, kind="ExternalInput")
    if debug:
        dbg_st = nc.dram_tensor("dbg_st", [128, 128], F32,
                                kind="ExternalOutput")
        dbg_xtm = nc.dram_tensor("dbg_xtm", [128, NT * C], BF16,
                                 kind="ExternalOutput")
        dbg_xn = nc.dram_tensor("dbg_xn", [128, NT * C], BF16,
                                kind="ExternalOutput")
        dbg_q = nc.dram_tensor("dbg_q", [128, NT * C], BF16,
                               kind="ExternalOutput")
        dbg_k = nc.dram_tensor("dbg_k", [128, NT * C], BF16,
                               kind="ExternalOutput")
        dbg_vt = nc.dram_tensor("dbg_vt", [128, VT_COLS], BF16,
                                kind="ExternalOutput")
    y0 = nc.dram_tensor("y0", [T_LEN, C], F32, kind="ExternalOutput")
    y1 = nc.dram_tensor("y1", [T_LEN, C], F32, kind="ExternalOutput")
    wt8 = nc.dram_tensor("wt8", [C, C], FP8)

    with ExitStack() as ctx:
        en = ctx.enter_context
        DW = Ctr(en(nc.semaphore("DW")))
        DY = Ctr(en(nc.semaphore("DY")))
        DO = Ctr(en(nc.semaphore("DO")))
        ST = Ctr(en(nc.semaphore("ST")))
        SV = Ctr(en(nc.semaphore("SV")))
        SS = Ctr(en(nc.semaphore("SS")))
        SG = Ctr(en(nc.semaphore("SG")))

        res = en(nc.sbuf_tensor("res", [128, NT * C], F32))
        B1 = en(nc.sbuf_tensor("B1", [128, NT * C], BF16))
        B2 = en(nc.sbuf_tensor("B2", [128, NT * C], BF16))
        KF = en(nc.sbuf_tensor("KF", [128, NT * C], BF16))
        VT = en(nc.sbuf_tensor("VT", [128, VT_COLS], BF16))
        WS = en(nc.sbuf_tensor("WS", [128, 2 * 1024], BF16))
        idb = en(nc.sbuf_tensor("idb", [128, 128], BF16))
        maskb = en(nc.sbuf_tensor("maskb", [128, 128], BF16))
        onesr = en(nc.sbuf_tensor("onesr", [128, 128], BF16))
        stats = en(nc.sbuf_tensor("stats", [128, 128], F32))
        bo_bf = en(nc.sbuf_tensor("bo_bf", [128, 1024], BF16))
        foldt = en(nc.sbuf_tensor("foldt", [128, 1024], BF16))
        c128 = en(nc.sbuf_tensor("c128", [128, 128], BF16))
        bq_t = en(nc.sbuf_tensor("bq_t", [128, 8], F32))
        bkd_t = en(nc.sbuf_tensor("bkd_t", [128, 4], F32))
        bkuk_t = en(nc.sbuf_tensor("bkuk_t", [128, 8], F32))
        onesf = en(nc.sbuf_tensor("onesf", [128, 128], F32))

        PS0 = en(nc.psum_tensor("PS0", [128, 1024], F32))
        PS1 = en(nc.psum_tensor("PS1", [128, 1024], F32))
        PA0 = en(nc.psum_tensor("PA0", [128, 512], F32))
        PA1 = en(nc.psum_tensor("PA1", [128, 512], F32))
        PT0 = en(nc.psum_tensor("PT0", [128, 1024], BF16))
        PT1 = en(nc.psum_tensor("PT1", [128, 1024], BF16))

        s_sumx = stats[:, 0:16]
        s_sumxx = stats[:, 16:32]
        s_m = stats[:, 32:48]
        s_var = stats[:, 48:64]
        s_r = stats[:, 64:80]
        s_nmr = stats[:, 80:96]
        s_eps = stats[:, 96:97]
        s_scale = stats[:, 97:98]
        s_rscale = stats[:, 98:99]
        s_tot = stats[:, 99:100]
        s_rden = stats[:, 100:108]
        s_msq = stats[:, 108:124]
        s_zero = stats[:, 124:125]

        B1f = B1.bitcast(F32)
        KFf = KF.bitcast(F32)
        VTf = VT.bitcast(F32)
        VT8 = VT.bitcast(FP8)
        B1i = B1.bitcast(I32)

        pb = [B2[:, 0:1024], B2[:, 1024:2048]]
        avfm = B2[:, 2048:6144]
        avn = [B2[:, 6144:6400], B2[:, 6400:6656]]
        wo_res = B2[:, 6656:14848]
        bias_row = B2[:, 14848:15872]
        scr1 = B2[:, 0:1024]
        scr2 = B2[:, 1024:2048]

        kb = KB()
        res3 = bass.AP(res, 0, [[NT * C, 128], [C, NT], [1, C]])

        def dma_g(dst_ap, src_ap, guard=None):
            if guard is not None:
                kb.wait("g", guard)
            DW.n += 16
            kb.emit("g", lambda e: e.dma_start(dst_ap, src_ap).then_inc(DW.h, 16))

        def dma_y(dst_ap, src_ap, guard=None):
            if guard is not None:
                kb.wait("y", guard)
            DY.n += 16
            kb.emit("y", lambda e: e.dma_start(dst_ap, src_ap).then_inc(DY.h, 16))

        def col_slices_hbm(w2d, co, rows, base_col=0):
            a = base_col + co * 128
            return w2d[:, a:a + 128].rearrange("(ci p) co -> p ci co", p=128)

        def whole_hbm(w2d):
            return w2d.rearrange("(ci p) co -> p ci co", p=128)

        gen_banks = [PS0[:, 0:512], PS0[:, 512:1024],
                     PS1[:, 0:512], PS1[:, 512:1024]]
        gb = {"i": 0, "free": [(0, None)] * 4}

        def next_bank():
            i = gb["i"] % 4
            gb["i"] += 1
            v, ctr = gb["free"][i]
            if ctr is not None:
                kb.wait("t", ctr, v)
            return i, gen_banks[i]

        def free_bank(i, ctr):
            gb["free"][i] = (ctr.n, ctr)

        kb.op("g", lambda e: e.iota(B1i[:, 0:128], [[-1, 128]], base=0,
                                    channel_multiplier=1), inc=SG)
        kb.wait("v", SG)
        kb.op("v", lambda e: e.tensor_scalar(idb[:], B1i[:, 0:128], 0.0, None,
                                             ALU.is_equal))
        kb.op("v", lambda e: e.tensor_scalar(maskb[:], B1i[:, 0:128], 0.0, None,
                                             ALU.is_le))
        kb.op("v", lambda e: e.memset(onesr[0:1, :], 1.0))
        kb.op("v", lambda e: e.memset(s_eps, EPS))
        kb.op("v", lambda e: e.memset(s_zero, 0.0))
        kb.op("v", lambda e: e.memset(onesf[:], 1.0))
        kb.op("v", lambda e: e.memset(c128[:], 1.0 / 128.0))
        vt_ones = bass.AP(VT, 64, [[VT_COLS, 128], [65, 256]])
        kb.op("v", lambda e: e.memset(vt_ones, 1.0))
        kb.dinc("v", SV)
        dma_y(bq_t[:], bq_h[:])
        dma_y(bkd_t[:], bkd_h[:])
        dma_y(bkuk_t[:], bkuk_h[:])
        dma_g(foldt[:], bkuv_h[:])
        dma_g(bo_bf[:], bo_h[:])
        dma_y(res3, x0.rearrange("(t p) c -> p t c", p=128))
        setup_dy = DY.n
        setup_dw = DW.n

        def ln_and_transpose(first):
            kb.wait("s", ST)
            kb.wait("v", ST)
            if first:
                kb.wait("v", DY, setup_dy)
                kb.wait("s", DY, setup_dy)
                kb.wait("t", DW, setup_dw)
            kb.wait("s", SV)
            for t in range(NT):
                rt = res[:, t * C:(t + 1) * C]
                kb.op("v", lambda e, rt=rt, t=t: e.tensor_scalar(
                    scr1[:], rt, 1.0, None, ALU.mult, op1=ALU.add,
                    accum_out=s_sumx[:, t:t + 1]))
                kb.op("s", lambda e, rt=rt, t=t: e.activation(
                    scr2[:], rt, AF.Square, accum_out=s_sumxx[:, t:t + 1]))
            kb.dinc("v", SV)
            kb.dinc("s", SS)
            kb.wait("v", SS)
            kb.op("v", lambda e: e.tensor_scalar(s_m, s_sumx, 1.0 / C, None,
                                                 ALU.mult))
            kb.dinc("v", SV)
            kb.op("v", lambda e: e.tensor_tensor(s_msq, s_m, s_m, ALU.mult))
            kb.dinc("v", SV)
            kb.op("v", lambda e: e.scalar_tensor_tensor(
                s_var, s_sumxx, 1.0 / C, s_msq, ALU.mult, ALU.subtract))
            kb.dinc("v", SV)
            kb.wait("s", SV)
            kb.op("s", lambda e: e.activation(s_r, s_var, AF.Ln,
                                              bias=s_eps, scale=1.0))
            kb.dinc("s", SS)
            kb.op("s", lambda e: e.activation(s_r, s_r, AF.Exp, scale=-0.5))
            kb.dinc("s", SS)
            kb.wait("v", SS)
            kb.op("v", lambda e: e.scalar_tensor_tensor(
                s_nmr, s_m, -1.0, s_r, ALU.mult, ALU.mult))
            kb.dinc("v", SV)
            for t in range(NT):
                rt = res[:, t * C:(t + 1) * C]
                xt = B1[:, t * C:(t + 1) * C]
                kb.op("v", lambda e, rt=rt, xt=xt, t=t: e.tensor_scalar(
                    xt, rt, s_r[:, t:t + 1], s_nmr[:, t:t + 1],
                    ALU.mult, ALU.add))
            kb.dinc("v", SV)
            kb.wait("t", SV)
            kb.tmode("trans")
            for t in range(NT):
                PT = PT0 if t % 2 == 0 else PT1
                if t >= 2:
                    kb.wait("t", SV)
                for ch in range(NCH):
                    src = B1[:, t * C + ch * 128: t * C + (ch + 1) * 128]
                    dst = PT[:, ch * 128:(ch + 1) * 128]
                    kb.op("t", lambda e, d=dst, s_=src: e.transpose(
                        d, s_, idb[:]), inc=(ST if ch == NCH - 1 else None))
                kb.wait("v", ST)
                src = bass.AP(PT, 0, [[1024, 128], [128, NCH], [1, 128]])
                dst = bass.AP(B2, t * 128, [[NT * C, 128], [T_LEN, NCH],
                                            [1, 128]])
                kb.op("v", lambda e, d=dst, s_=src: e.tensor_copy(d, s_))
                kb.dinc("v", SV)
            kb.wait("t", SV)

        def proj_wstat(w_hbm, n_ci, n_co, rhs_col0, evac_fn, w_rows,
                       base_col=0):
            slot = [WS[:, 0:1024], WS[:, 1024:2048]]
            slot_rows = w_rows // 128
            mm_done = {}
            dw_at = {}
            def slot3(i):
                return bass.AP(WS, i * 1024,
                               [[2048, 128], [128, slot_rows], [1, 128]])

            for co in range(min(2, n_co)):
                dma_g(slot3(co % 2), col_slices_hbm(w_hbm, co, w_rows,
                                                    base_col), guard=ST)
                dw_at[co] = DW.n
            kb.tmode("full")
            for co in range(n_co):
                kb.wait("t", DW, dw_at[co])
                sl = slot[co % 2]
                for qb in range(NQB):
                    bi, bank = next_bank()
                    for ci in range(n_ci):
                        lhsT = sl[:, ci * 128:(ci + 1) * 128]
                        rhs = rhs_col0(ci, qb)
                        kb.op("t", lambda e, b=bank, l=lhsT, r=rhs,
                              st=(ci == 0), sp=(ci == n_ci - 1): e.matmul(
                                  b, l, r, start=st, stop=sp,
                                  skip_group_check=True),
                              inc=(ST if ci == n_ci - 1 else None))
                    evac_fn(co, qb, bank, bi)
                mm_done[co] = ST.n
                nxt = co + 2
                if nxt < n_co:
                    kb.wait("g", ST, mm_done[co])
                    dma_g(slot3(nxt % 2),
                          col_slices_hbm(w_hbm, nxt, w_rows, base_col))
                    dw_at[nxt] = DW.n

        def evac_to_fm(dst_tensor, dst_base, eng, sem, bias_col=None,
                       func=AF.Identity):
            def f(co, qb, bank, bi):
                dst = dst_tensor[:, dst_base + co * T_LEN + qb * 512:
                                 dst_base + co * T_LEN + qb * 512 + 512]
                kb.wait(eng, ST)
                if eng == "s":
                    b = (bias_col[:, co:co + 1] if bias_col is not None
                         else s_zero)
                    kb.op("s", lambda e, d=dst, p=bank, b=b: e.activation(
                        d, p, func, bias=b, scale=1.0), inc=sem)
                else:
                    kb.op("v", lambda e, d=dst, p=bank: e.tensor_copy(d, p),
                          inc=sem)
                free_bank(bi, sem)
            return f

        def proj_v(lhs_col0, n_ci, rhs_res):
            kb.tmode("full")
            for qt in range(NT):
                for eh in range(2):
                    bi, bank = next_bank()
                    for ci in range(n_ci):
                        lhsT = lhs_col0(ci, qt)
                        rhs = rhs_res[:, ci * 1024 + eh * 512:
                                      ci * 1024 + eh * 512 + 512]
                        kb.op("t", lambda e, b=bank, l=lhsT, r=rhs,
                              st=(ci == 0), sp=(ci == n_ci - 1): e.matmul(
                                  b, l, r, start=st, stop=sp,
                                  skip_group_check=True),
                              inc=(ST if ci == n_ci - 1 else None))
                    kb.wait("v", ST)
                    dst = bass.AP(VT, (qt * 16 + eh * 8) * 65,
                                  [[VT_COLS, 128], [65, 8], [1, 64]])
                    src = bass.AP(bank.tensor, bank.offset,
                                  [[bank.tensor.shape[1], 128], [64, 8],
                                   [1, 64]])
                    kb.op("v", lambda e, d=dst, s_=src: e.tensor_copy(d, s_),
                          inc=SV)
                    free_bank(bi, SV)

        def attention(q_t, k_t, with_bias_row):
            kc_par = {"i": 0}
            ps_exp = [0, 0]
            ps_sv = [0, 0]
            av_free_sv = SV.n

            for qb in range(NQB):
                n_kc = 4 * (qb + 1)
                for hp in range(NHP):
                    for kc in range(n_kc):
                        buf = kc_par["i"] % 2
                        kc_par["i"] += 1
                        PS = PS0 if buf == 0 else PS1
                        diag = kc >= 4 * qb
                        off = (kc - 4 * qb) * 128 if diag else 0
                        w = 512 - off
                        kb.tmode("row64")
                        kb.wait("t", SS, ps_exp[buf])
                        if ps_sv[buf]:
                            kb.wait("t", SV, ps_sv[buf])
                            ps_sv[buf] = 0
                        for h2 in range(2):
                            rb = h2 * 64
                            kcol = hp * T_LEN + kc * 128
                            qcol = hp * T_LEN + qb * 512 + off
                            lhsT = k_t[rb:rb + 64, kcol:kcol + 128]
                            rhs = q_t[rb:rb + 64, qcol:qcol + w]
                            out = PS[:, h2 * 512 + off: h2 * 512 + off + w]
                            kb.op("t", lambda e, o=out, l=lhsT, r=rhs, rb=rb:
                                  e.matmul(o, l, r, start=True, stop=True,
                                           skip_group_check=True,
                                           tile_position=(rb, 0)),
                                  inc=(ST if h2 == 1 else None))
                        kb.wait("s", ST)
                        src = bass.AP(PS, off, [[1024, 128], [512, 2], [1, w]])
                        dst = bass.AP(pb[buf].tensor, pb[buf].offset + off,
                                      [[NT * C, 128], [512, 2], [1, w]])
                        kb.op("s", lambda e, d=dst, s_=src: e.activation(
                            d, s_, AF.Exp, scale=ATT_SCALE))
                        kb.dinc("s", SS)
                        ps_exp[buf] = SS.n
                        if diag:
                            kb.wait("v", SS)
                            mt = bass.AP(pb[buf].tensor, pb[buf].offset + off,
                                         [[NT * C, 128], [512, 2], [1, 128]])
                            mb = bass.AP(maskb, 0,
                                         [[128, 128], [0, 2], [1, 128]])
                            kb.op("v", lambda e, m=mt, mb=mb: e.tensor_tensor(
                                m, m, mb, ALU.mult))
                            kb.dinc("v", SV)
                            mask_sv = SV.n
                        if diag:
                            kb.wait("t", SV, mask_sv)
                        else:
                            kb.wait("t", SS)
                        if kc == 0 and av_free_sv:
                            kb.wait("t", SV, av_free_sv)
                        kb.tmode("full")
                        qt_lo = max(0, kc - 4 * qb)
                        for h2 in range(2):
                            h = 2 * hp + h2
                            bank = PA0 if h2 == 0 else PA1
                            for qtl in range(qt_lo, 4):
                                lhsT = pb[buf][:, h2 * 512 + qtl * 128:
                                               h2 * 512 + (qtl + 1) * 128]
                                rhs = bass.AP(VT, (kc * 16 + h) * 65,
                                              [[VT_COLS, 128], [1, 65]])
                                out = bank[:, qtl * 65: qtl * 65 + 65]
                                first = (kc == 0 and qtl == 0)
                                last = (kc == n_kc - 1 and qtl == 3 and
                                        h2 == 1)
                                kb.op("t", lambda e, o=out, l=lhsT, r=rhs,
                                      st=first: e.matmul(
                                          o, l, r, start=st, stop=False,
                                          skip_group_check=True),
                                      inc=(ST if last else None))
                    kb.wait("v", ST)
                    for h2 in range(2):
                        bank = PA0 if h2 == 0 else PA1
                        for qtl in range(4):
                            dcol = bank[:, qtl * 65 + 64: qtl * 65 + 65]
                            ix = h2 * 4 + qtl
                            kb.op("v", lambda e, d=dcol, ix=ix: e.reciprocal(
                                s_rden[:, ix:ix + 1], d))
                    kb.dinc("v", SV)
                    for h2 in range(2):
                        bank = PA0 if h2 == 0 else PA1
                        for qtl in range(4):
                            ix = h2 * 4 + qtl
                            src = bank[:, qtl * 65: qtl * 65 + 64]
                            dst = avn[h2][:, qtl * 64:(qtl + 1) * 64]
                            kb.op("v", lambda e, d=dst, s_=src, ix=ix:
                                  e.tensor_scalar(d, s_,
                                                  s_rden[:, ix:ix + 1],
                                                  None, ALU.mult))
                    kb.dinc("v", SV)
                    av_free_sv = SV.n
                    PT = PT0 if hp % 2 == 0 else PT1
                    kb.wait("t", SV)
                    kb.tmode("trans")
                    for h2 in range(2):
                        for qtl in range(4):
                            src = avn[h2][:, qtl * 64:(qtl + 1) * 64]
                            dst = PT[h2 * 64:h2 * 64 + 64,
                                     qtl * 128:(qtl + 1) * 128]
                            last = (h2 == 1 and qtl == 3)
                            kb.op("t", lambda e, d=dst, s_=src, h2=h2:
                                  e.transpose(d, s_, idb[:],
                                              tile_position=(0, h2 * 64)),
                                  inc=(ST if last else None))
                    kb.wait("v", ST)
                    dst = avfm[:, hp * 512:(hp + 1) * 512]
                    kb.op("v", lambda e, d=dst, s_=PT: e.tensor_copy(
                        d, s_[:, 0:512]))
                    kb.dinc("v", SV)
                kb.wait("t", SV)
                kb.tmode("full")
                for qtl in range(4):
                    qt_g = qb * 4 + qtl
                    for ob in range(2):
                        bi, bank = next_bank()
                        for ch in range(NCH):
                            lhsT = avfm[:, ch * 512 + qtl * 128:
                                        ch * 512 + (qtl + 1) * 128]
                            rhs = wo_res[:, ch * 1024 + ob * 512:
                                         ch * 1024 + ob * 512 + 512]
                            last_mm = (ch == NCH - 1) and not with_bias_row
                            kb.op("t", lambda e, b=bank, l=lhsT, r=rhs,
                                  st=(ch == 0), sp=last_mm: e.matmul(
                                      b, l, r, start=st, stop=sp,
                                      skip_group_check=True),
                                  inc=(ST if last_mm else None))
                        if with_bias_row:
                            rhs = bias_row[:, ob * 512:(ob + 1) * 512]
                            kb.op("t", lambda e, b=bank, l=c128[:], r=rhs:
                                  e.matmul(b, l, r, start=False, stop=True,
                                           skip_group_check=True), inc=ST)
                        kb.wait("v", ST)
                        dstr = res[:, qt_g * C + ob * 512:
                                   qt_g * C + ob * 512 + 512]
                        kb.op("v", lambda e, d=dstr, p=bank: e.tensor_tensor(
                            d, d, p, ALU.add), inc=SV)
                        free_bank(bi, SV)
                ps_sv[0] = SV.n
                ps_sv[1] = SV.n
            kb.dinc("v", SV)

        def attn_a():
            ln_and_transpose(first=(sub_idx[0] == 0))
            if debug:
                kb.wait("y", SV)
                dma_y(dbg_st[:], stats[:])
                dma_y(dbg_xtm[:], B1[:])

            def xnT_rhs(ci, qb):
                return B2[:, ci * T_LEN + qb * 512: ci * T_LEN + qb * 512 + 512]

            def xnT_lhs(ci, qt):
                return B2[:, ci * T_LEN + qt * 128: ci * T_LEN + qt * 128 + 128]

            kb.wait("v", ST)
            kb.op("v", lambda e: e.memset(vt_ones, 1.0))
            kb.dinc("v", SV)
            proj_wstat(wkT, NCH, NCH, xnT_rhs,
                       evac_to_fm(KF, 0, "s", SS), C)
            kb.dinc("s", SS)
            dma_g(B1.reshape([128, 16, 1024])[:, 0:8, :], whole_hbm(wvT),
                  guard=ST)
            wv_dw = DW.n
            kb.wait("t", DW, wv_dw)
            proj_v(xnT_lhs, NCH, B1[:, 0:8192])
            proj_wstat(wqT, NCH, NCH, xnT_rhs,
                       evac_to_fm(B1, 0, "v", SV), C)
            kb.dinc("v", SV)
            dst = bass.AP(B2, 6656, [[NT * C, 128], [1024, 8], [1, 1024]])
            dma_g(dst, whole_hbm(woT), guard=ST)
            kb.wait("t", DW)
            kb.wait("t", SV)
            kb.wait("t", SS)
            if debug:
                kb.wait("y", SV)
                kb.wait("y", SS)
                dma_y(dbg_xn[:], B2[:])
                dma_y(dbg_q[:], B1[:])
                dma_y(dbg_k[:], KF[:])
                dma_y(dbg_vt[:], VT[:])
            attention(B1, KF, with_bias_row=False)

        def ffn():
            ln_and_transpose(first=False)

            def xnT_lhs(ci, qt):
                return B2[:, ci * T_LEN + qt * 128: ci * T_LEN + qt * 128 + 128]

            if not ffn_state["q"]:
                dma_y(bass.AP(KFf, 0, [[8192, 128], [1024, 8], [1, 1024]]),
                      whole_hbm(fwT), guard=ST)
                kb.wait("v", DY)
                kb.wait("v", ST)
                w_all = KFf[:, 0:8192]
                kb.op("v", lambda e: e.tensor_reduce(
                    s_tot, w_all, AXL.X, ALU.add, apply_absolute_value=True))
                kb.dinc("v", SV)
                kb.wait("t", SV)
                kb.tmode("fp32a")
                kb.op("t", lambda e: e.matmul(
                    PA0[0:1, 0:1], onesf[:, 0:1], s_tot, start=True,
                    stop=True, skip_group_check=True), inc=ST)
                kb.wait("v", ST)
                kb.op("v", lambda e: e.tensor_copy(s_msq[0:1, 0:1],
                                                   PA0[0:1, 0:1]))
                kb.dinc("v", SV)
                kb.wait("t", SV)
                kb.tmode("fp32b")
                kb.op("t", lambda e: e.matmul(
                    PA1[:, 0:1], onesf[0:1, :], s_msq[0:1, 0:1], start=True,
                    stop=True, skip_group_check=True), inc=ST)
                kb.wait("v", ST)
                kb.op("v", lambda e: e.tensor_scalar(
                    s_scale, PA1[:, 0:1], 1.0 / (C * C), 1e-5,
                    ALU.mult, ALU.max))
                kb.dinc("v", SV)
                kb.op("v", lambda e: e.reciprocal(s_rscale, s_scale))
                kb.dinc("v", SV)
                u = VTf[:, 0:8192]
                kb.op("v", lambda e: e.tensor_scalar(
                    u, w_all, s_rscale[:, 0:1], None, ALU.mult))
                kb.dinc("v", SV)
                g_ = B1f[:, 0:8192]
                l_ = KFf[:, 0:8192]
                kb.op("v", lambda e: e.tensor_scalar(g_, u, 0.5, None,
                                                     ALU.is_gt))
                kb.op("v", lambda e: e.tensor_scalar(l_, u, -0.5, None,
                                                     ALU.is_lt))
                kb.dinc("v", SV)
                kb.op("v", lambda e: e.tensor_tensor(
                    VT8[:, 0:8192], g_, l_, ALU.subtract))
                kb.dinc("v", SV)
                kb.wait("y", SV)
                dma_y(whole_hbm(wt8),
                      bass.AP(VT8, 0, [[2 * VT_COLS, 128], [1024, 8],
                                       [1, 1024]]))
                ffn_state["q"] = True
                ffn_state["store_dy"] = DY.n
            else:
                dma_y(bass.AP(VT8, 0, [[2 * VT_COLS, 128], [1024, 8],
                                       [1, 1024]]),
                      whole_hbm(wt8), guard=ST)
            wt_dy = DY.n
            kb.wait("t", DY, wt_dy)
            for qt in range(NT):
                for ob in range(2):
                    bi, bank = next_bank()
                    for ci in range(NCH):
                        lhsT = xnT_lhs(ci, qt)
                        rhs = VT8[:, ci * 1024 + ob * 512:
                                  ci * 1024 + ob * 512 + 512]
                        kb.op("t", lambda e, b=bank, l=lhsT, r=rhs,
                              st=(ci == 0), sp=(ci == NCH - 1): e.matmul(
                                  b, l, r, start=st, stop=sp,
                                  skip_group_check=True),
                              inc=(ST if ci == NCH - 1 else None))
                    kb.wait("v", ST)
                    dstr = res[:, qt * C + ob * 512: qt * C + ob * 512 + 512]
                    kb.op("v", lambda e, d=dstr, p=bank:
                          e.scalar_tensor_tensor(d, p, s_scale[:, 0:1], d,
                                                 ALU.mult, ALU.add), inc=SV)
                    free_bank(bi, SV)
            kb.dinc("v", SV)
            if ffn_state.get("store_dy"):
                kb.wait("v", DY, ffn_state["store_dy"])
                ffn_state["store_dy"] = 0

        def attn_b():
            ln_and_transpose(first=False)

            def xnT_rhs(ci, qb):
                return B2[:, ci * T_LEN + qb * 512: ci * T_LEN + qb * 512 + 512]

            kb.wait("v", ST)
            kb.op("v", lambda e: e.memset(vt_ones, 1.0))
            kb.dinc("v", SV)
            proj_wstat(akd, NCH, 4, xnT_rhs,
                       evac_to_fm(B1, 0, "s", SS, bias_col=bkd_t,
                                  func=AF.Gelu), C)
            kb.dinc("s", SS)
            kb.wait("v", SS)
            for lc in range(2):
                dst = B1[:, (4 + lc) * T_LEN:(5 + lc) * T_LEN]
                a_ = B1[:, lc * T_LEN:(lc + 1) * T_LEN]
                b_ = B1[:, (2 + lc) * T_LEN:(3 + lc) * T_LEN]
                kb.op("v", lambda e, d=dst, a=a_, b=b_: e.tensor_tensor(
                    d, a, b, ALU.add))
            kb.dinc("v", SV)

            def lat_rhs(ci, qb):
                base = (4 + ci) * T_LEN + qb * 512
                return B1[:, base:base + 512]

            def lat_lhs(ci, qt):
                base = (4 + ci) * T_LEN + qt * 128
                return B1[:, base:base + 128]

            kb.wait("t", SV)
            proj_wstat(aku, 2, NCH, lat_rhs,
                       evac_to_fm(KF, 0, "s", SS, bias_col=bkuk_t,
                                  func=AF.Identity), LAT, base_col=0)
            kb.dinc("s", SS)
            dma_g(B1.reshape([128, 16, 1024])[:, 0:2, :],
                  aku[:, C:2 * C].rearrange("(ci p) co -> p ci co", p=128),
                  guard=SV)
            kb.wait("t", DW)
            kb.wait("t", SS)
            proj_v(lat_lhs, 2, B1[:, 0:2048])
            proj_wstat(aq, NCH, NCH, xnT_rhs,
                       evac_to_fm(B1, 0, "s", SS, bias_col=bq_t,
                                  func=AF.Identity), C)
            kb.dinc("s", SS)
            dst = bass.AP(B2, 6656, [[NT * C, 128], [1024, 8], [1, 1024]])
            dma_g(dst, whole_hbm(awo), guard=ST)
            kb.wait("t", DW)
            kb.tmode("full")
            for ob in range(2):
                bank = PA0[:, 0:512] if ob == 0 else PA1[:, 0:512]
                for ci in range(NCH):
                    lhsT = foldt[:, ci * 128:(ci + 1) * 128]
                    rhs = wo_res[:, ci * 1024 + ob * 512:
                                 ci * 1024 + ob * 512 + 512]
                    kb.op("t", lambda e, b=bank, l=lhsT, r=rhs, st=(ci == 0),
                          sp=(ci == NCH - 1): e.matmul(
                              b, l, r, start=st, stop=sp,
                              skip_group_check=True),
                          inc=(ST if ci == NCH - 1 else None))
                kb.wait("v", ST)
                dst2 = bias_row[:, ob * 512:(ob + 1) * 512]
                kb.op("v", lambda e, d=dst2, p=bank, ob=ob: e.tensor_tensor(
                    d, p, bo_bf[:, ob * 512:(ob + 1) * 512], ALU.add))
            kb.dinc("v", SV)
            kb.wait("t", SV)
            kb.wait("t", SS)
            attention(B1, KF, with_bias_row=True)

        ffn_state = {"q": False, "store_dy": 0}
        sub_idx = [0]
        phases = [attn_a, ffn, attn_b, ffn]
        for seq in range(2):
            if sub_idx[0] >= n_sublayers:
                break
            if seq == 1:
                kb.wait("y", SV)
                DO.n += 16
                kb.emit("y", lambda e: e.dma_start(
                    y0.rearrange("(t p) c -> p t c", p=128),
                    res3).then_inc(DO.h, 16))
                kb.wait("y", DO)
                dma_y(res3,
                      x1.rearrange("(t p) c -> p t c", p=128))
                xdy = DY.n
                kb.wait("v", DY, xdy)
                kb.wait("s", DY, xdy)
            for ph in phases:
                if sub_idx[0] >= n_sublayers:
                    break
                ph()
                sub_idx[0] += 1
        ran_seq1 = sub_idx[0] > 4
        kb.wait("y", SV)
        if ran_seq1:
            DO.n += 16
            kb.emit("y", lambda e: e.dma_start(
                y1.rearrange("(t p) c -> p t c", p=128),
                res3).then_inc(DO.h, 16))
        else:
            DO.n += 16
            kb.emit("y", lambda e: e.dma_start(
                y0.rearrange("(t p) c -> p t c", p=128),
                res3).then_inc(DO.h, 16))
            DO.n += 16
            kb.emit("y", lambda e: e.dma_start(
                y1.rearrange("(t p) c -> p t c", p=128),
                res3).then_inc(DO.h, 16))
        kb.wait("y", DO)

        with nc.Block() as block:
            @block.tensor
            def _(t):
                for f in kb.ops["t"]:
                    f(t)

            @block.vector
            def _(v):
                for f in kb.ops["v"]:
                    f(v)

            @block.scalar
            def _(s):
                for f in kb.ops["s"]:
                    f(s)

            @block.gpsimd
            def _(g):
                for f in kb.ops["g"]:
                    f(g)

            @block.sync
            def _(y):
                for f in kb.ops["y"]:
                    f(y)

    return nc


def _shared_inputs(inputs):
    d = inputs

    def a(x):
        return np.ascontiguousarray(np.asarray(x, dtype=np.float32))

    return {
        "wqT": a(np.asarray(d["L0_attn_a_wq"]).T),
        "wkT": a(np.asarray(d["L0_attn_a_wk"]).T),
        "wvT": a(np.asarray(d["L0_attn_a_wv"]).T),
        "woT": a(np.asarray(d["L0_attn_a_wo"]).T),
        "aq": a(d["L0_attn_b_Wq"]),
        "akd": a(d["L0_attn_b_Wkd"]),
        "aku": a(d["L0_attn_b_Wku"]),
        "awo": a(d["L0_attn_b_Wo"]),
        "fwT": a(np.asarray(d["L0_ffn_weight"]).T),
        "bq": a(np.asarray(d["L0_attn_b_bq"]).reshape(8, 128).T),
        "bkd": a(np.asarray(d["L0_attn_b_bkd"]).reshape(4, 128).T),
        "bkuk": a(np.asarray(d["L0_attn_b_bku"])[:1024].reshape(8, 128).T),
        "bkuv": a(np.broadcast_to(
            np.asarray(d["L0_attn_b_bku"])[1024:].reshape(8, 128).T[:, :, None],
            (128, 8, 128)).reshape(128, 1024)),
        "bo": a(np.broadcast_to(
            np.asarray(d["L0_attn_b_bo"])[None, :], (128, 1024))),
    }


def kernel(**inputs):
    nc = build_program()
    sh = _shared_inputs(inputs)
    x = np.asarray(inputs["L0_attn_a_x"], dtype=np.float32)
    in_maps = []
    for core in range(8):
        m = dict(sh)
        m["x0"] = np.ascontiguousarray(x[2 * core])
        m["x1"] = np.ascontiguousarray(x[2 * core + 1])
        in_maps.append(m)
    res = run_bass_kernel_spmd(nc, in_maps, core_ids=list(range(8)))
    out = np.empty((16, T_LEN, C), dtype=np.float32)
    for core in range(8):
        out[2 * core] = res.results[core]["y0"]
        out[2 * core + 1] = res.results[core]["y1"]
    return out


# version 03
# speedup vs baseline: 1.1407x; 1.0512x over previous
import numpy as np
import concourse.bass as bass
import concourse.mybir as mybir
import bass_rust as _br
from concourse.bass_utils import run_bass_kernel_spmd
from contextlib import ExitStack

F32 = mybir.dt.float32
BF16 = mybir.dt.bfloat16
FP8 = mybir.dt.float8e4
I32 = mybir.dt.int32
AF = mybir.ActivationFunctionType
ALU = mybir.AluOpType
AXL = mybir.AxisListType

T_LEN = 2048
C = 1024
NT = 16
NCH = 8
NQB = 4
NHP = 8
LAT = 256
EPS = 1e-5
ATT_SCALE = 0.125
VT_COLS = 16 * 16 * 65


class Ctr:
    def __init__(self, h):
        self.h = h
        self.n = 0


class KB:
    ENGS = ("t", "v", "s", "g", "y")

    def __init__(self):
        self.ops = {e: [] for e in self.ENGS}
        self.tmode_cur = [None]

    def emit(self, eng, fn):
        self.ops[eng].append(fn)

    def wait(self, eng, ctr, val=None):
        v = ctr.n if val is None else val
        if v <= 0:
            return
        h = ctr.h
        self.emit(eng, lambda e: e.wait_ge(h, v))

    def op(self, eng, fn, inc=None):
        if inc is None:
            self.emit(eng, fn)
        elif eng == "t":
            inc.n += 1
            h = inc.h
            self.emit(eng, lambda e: fn(e).then_inc(h, 1))
        else:
            inc.n += 1
            h = inc.h
            self.emit(eng, fn)
            self.emit(eng, lambda e: e.drain(fusable=False).then_inc(h, 1))

    def dinc(self, eng, ctr):
        ctr.n += 1
        h = ctr.h
        self.emit(eng, lambda e: e.drain(fusable=False).then_inc(h, 1))

    TMODE_DRAINS = False
    T_DRAIN_INCS = False

    def tmode(self, m):
        if self.tmode_cur[0] != m:
            if self.TMODE_DRAINS and self.tmode_cur[0] is not None:
                self.emit("t", lambda e: e.drain(fusable=False))
            self.tmode_cur[0] = m


def build_program(n_sublayers=8, debug=False):
    nc = bass.Bass("TRN2", target_bir_lowering=False)

    x0 = nc.dram_tensor("x0", [T_LEN, C], F32, kind="ExternalInput")
    x1 = nc.dram_tensor("x1", [T_LEN, C], F32, kind="ExternalInput")
    wqT = nc.dram_tensor("wqT", [C, C], F32, kind="ExternalInput")
    wkT = nc.dram_tensor("wkT", [C, C], F32, kind="ExternalInput")
    wvT = nc.dram_tensor("wvT", [C, C], F32, kind="ExternalInput")
    woT = nc.dram_tensor("woT", [C, C], F32, kind="ExternalInput")
    aq = nc.dram_tensor("aq", [C, C], F32, kind="ExternalInput")
    akd = nc.dram_tensor("akd", [C, 512], F32, kind="ExternalInput")
    aku = nc.dram_tensor("aku", [LAT, 2 * C], F32, kind="ExternalInput")
    awo = nc.dram_tensor("awo", [C, C], F32, kind="ExternalInput")
    fwT = nc.dram_tensor("fwT", [C, C], F32, kind="ExternalInput")
    bq_h = nc.dram_tensor("bq", [128, 8], F32, kind="ExternalInput")
    bkd_h = nc.dram_tensor("bkd", [128, 4], F32, kind="ExternalInput")
    bkuk_h = nc.dram_tensor("bkuk", [128, 8], F32, kind="ExternalInput")
    bkuv_h = nc.dram_tensor("bkuv", [128, 1024], F32, kind="ExternalInput")
    bo_h = nc.dram_tensor("bo", [128, C], F32, kind="ExternalInput")
    if debug:
        dbg_st = nc.dram_tensor("dbg_st", [128, 128], F32,
                                kind="ExternalOutput")
        dbg_xtm = nc.dram_tensor("dbg_xtm", [128, NT * C], BF16,
                                 kind="ExternalOutput")
        dbg_xn = nc.dram_tensor("dbg_xn", [128, NT * C], BF16,
                                kind="ExternalOutput")
        dbg_q = nc.dram_tensor("dbg_q", [128, NT * C], BF16,
                               kind="ExternalOutput")
        dbg_k = nc.dram_tensor("dbg_k", [128, NT * C], BF16,
                               kind="ExternalOutput")
        dbg_vt = nc.dram_tensor("dbg_vt", [128, VT_COLS], BF16,
                                kind="ExternalOutput")
    y0 = nc.dram_tensor("y0", [T_LEN, C], F32, kind="ExternalOutput")
    y1 = nc.dram_tensor("y1", [T_LEN, C], F32, kind="ExternalOutput")
    wt8 = nc.dram_tensor("wt8", [C, C], FP8)

    with ExitStack() as ctx:
        en = ctx.enter_context
        DW = Ctr(en(nc.semaphore("DW")))
        DY = Ctr(en(nc.semaphore("DY")))
        DO = Ctr(en(nc.semaphore("DO")))
        ST = Ctr(en(nc.semaphore("ST")))
        SV = Ctr(en(nc.semaphore("SV")))
        SS = Ctr(en(nc.semaphore("SS")))
        SG = Ctr(en(nc.semaphore("SG")))

        res = en(nc.sbuf_tensor("res", [128, NT * C], F32))
        B1 = en(nc.sbuf_tensor("B1", [128, NT * C], BF16))
        B2 = en(nc.sbuf_tensor("B2", [128, NT * C], BF16))
        KF = en(nc.sbuf_tensor("KF", [128, NT * C], BF16))
        VT = en(nc.sbuf_tensor("VT", [128, VT_COLS], BF16))
        WS = en(nc.sbuf_tensor("WS", [128, 2 * 1024], BF16))
        idb = en(nc.sbuf_tensor("idb", [128, 128], BF16))
        maskb = en(nc.sbuf_tensor("maskb", [128, 128], BF16))
        onesr = en(nc.sbuf_tensor("onesr", [128, 128], BF16))
        stats = en(nc.sbuf_tensor("stats", [128, 128], F32))
        bo_bf = en(nc.sbuf_tensor("bo_bf", [128, 1024], BF16))
        foldt = en(nc.sbuf_tensor("foldt", [128, 1024], BF16))
        c128 = en(nc.sbuf_tensor("c128", [128, 128], BF16))
        bq_t = en(nc.sbuf_tensor("bq_t", [128, 8], F32))
        bkd_t = en(nc.sbuf_tensor("bkd_t", [128, 4], F32))
        bkuk_t = en(nc.sbuf_tensor("bkuk_t", [128, 8], F32))
        onesf = en(nc.sbuf_tensor("onesf", [128, 128], F32))

        PS0 = en(nc.psum_tensor("PS0", [128, 1024], F32))
        PS1 = en(nc.psum_tensor("PS1", [128, 1024], F32))
        PA0 = en(nc.psum_tensor("PA0", [128, 512], F32))
        PA1 = en(nc.psum_tensor("PA1", [128, 512], F32))
        PT0 = en(nc.psum_tensor("PT0", [128, 1024], BF16))
        PT1 = en(nc.psum_tensor("PT1", [128, 1024], BF16))

        s_sumx = stats[:, 0:16]
        s_sumxx = stats[:, 16:32]
        s_m = stats[:, 32:48]
        s_var = stats[:, 48:64]
        s_r = stats[:, 64:80]
        s_nmr = stats[:, 80:96]
        s_eps = stats[:, 96:97]
        s_scale = stats[:, 97:98]
        s_rscale = stats[:, 98:99]
        s_tot = stats[:, 99:100]
        s_rden = stats[:, 100:108]
        s_msq = stats[:, 108:124]
        s_zero = stats[:, 124:125]

        B1f = B1.bitcast(F32)
        KFf = KF.bitcast(F32)
        VTf = VT.bitcast(F32)
        VT8 = VT.bitcast(FP8)
        B1i = B1.bitcast(I32)

        pb = [B2[:, 0:1024], B2[:, 1024:2048]]
        avfm = B2[:, 2048:6144]
        avn = [B2[:, 6144:6400], B2[:, 6400:6656]]
        wo_res = B2[:, 6656:14848]
        bias_row = B2[:, 14848:15872]
        scr1 = B2[:, 0:1024]
        scr2 = B2[:, 1024:2048]

        kb = KB()
        res3 = bass.AP(res, 0, [[NT * C, 128], [C, NT], [1, C]])

        def dma_g(dst_ap, src_ap, guard=None):
            if guard is not None:
                kb.wait("g", guard)
            DW.n += 16
            kb.emit("g", lambda e: e.dma_start(dst_ap, src_ap).then_inc(DW.h, 16))

        def dma_y(dst_ap, src_ap, guard=None):
            if guard is not None:
                kb.wait("y", guard)
            DY.n += 16
            kb.emit("y", lambda e: e.dma_start(dst_ap, src_ap).then_inc(DY.h, 16))

        def col_slices_hbm(w2d, co, rows, base_col=0):
            a = base_col + co * 128
            return w2d[:, a:a + 128].rearrange("(ci p) co -> p ci co", p=128)

        def whole_hbm(w2d):
            return w2d.rearrange("(ci p) co -> p ci co", p=128)

        gen_banks = [PS0[:, 0:512], PS0[:, 512:1024],
                     PS1[:, 0:512], PS1[:, 512:1024]]
        gb = {"i": 0, "free": [(0, None)] * 4}

        def next_bank():
            i = gb["i"] % 4
            gb["i"] += 1
            v, ctr = gb["free"][i]
            if ctr is not None:
                kb.wait("t", ctr, v)
            return i, gen_banks[i]

        def free_bank(i, ctr):
            gb["free"][i] = (ctr.n, ctr)

        kb.op("g", lambda e: e.iota(B1i[:, 0:128], [[-1, 128]], base=0,
                                    channel_multiplier=1), inc=SG)
        kb.wait("v", SG)
        kb.op("v", lambda e: e.tensor_scalar(idb[:], B1i[:, 0:128], 0.0, None,
                                             ALU.is_equal))
        kb.op("v", lambda e: e.tensor_scalar(maskb[:], B1i[:, 0:128], 0.0, None,
                                             ALU.is_le))
        kb.op("v", lambda e: e.memset(onesr[0:1, :], 1.0))
        kb.op("v", lambda e: e.memset(s_eps, EPS))
        kb.op("v", lambda e: e.memset(s_zero, 0.0))
        kb.op("v", lambda e: e.memset(onesf[:], 1.0))
        kb.op("v", lambda e: e.memset(c128[:], 1.0 / 128.0))
        vt_ones = bass.AP(VT, 64, [[VT_COLS, 128], [65, 256]])
        kb.op("v", lambda e: e.memset(vt_ones, 1.0))
        kb.dinc("v", SV)
        dma_y(bq_t[:], bq_h[:])
        dma_y(bkd_t[:], bkd_h[:])
        dma_y(bkuk_t[:], bkuk_h[:])
        dma_g(foldt[:], bkuv_h[:])
        dma_g(bo_bf[:], bo_h[:])
        dma_y(res3, x0.rearrange("(t p) c -> p t c", p=128))
        setup_dy = DY.n
        setup_dw = DW.n

        def ln_and_transpose(first):
            kb.wait("s", ST)
            kb.wait("v", ST)
            if first:
                kb.wait("v", DY, setup_dy)
                kb.wait("s", DY, setup_dy)
                kb.wait("t", DW, setup_dw)
            kb.wait("s", SV)
            for t in range(NT):
                rt = res[:, t * C:(t + 1) * C]
                kb.op("v", lambda e, rt=rt, t=t: e.tensor_scalar(
                    scr1[:], rt, 1.0, None, ALU.mult, op1=ALU.add,
                    accum_out=s_sumx[:, t:t + 1]))
                kb.op("s", lambda e, rt=rt, t=t: e.activation(
                    scr2[:], rt, AF.Square, accum_out=s_sumxx[:, t:t + 1]))
            kb.dinc("v", SV)
            kb.dinc("s", SS)
            kb.wait("v", SS)
            kb.op("v", lambda e: e.tensor_scalar(s_m, s_sumx, 1.0 / C, None,
                                                 ALU.mult))
            kb.dinc("v", SV)
            kb.op("v", lambda e: e.tensor_tensor(s_msq, s_m, s_m, ALU.mult))
            kb.dinc("v", SV)
            kb.op("v", lambda e: e.scalar_tensor_tensor(
                s_var, s_sumxx, 1.0 / C, s_msq, ALU.mult, ALU.subtract))
            kb.dinc("v", SV)
            kb.wait("s", SV)
            kb.op("s", lambda e: e.activation(s_r, s_var, AF.Ln,
                                              bias=s_eps, scale=1.0))
            kb.dinc("s", SS)
            kb.op("s", lambda e: e.activation(s_r, s_r, AF.Exp, scale=-0.5))
            kb.dinc("s", SS)
            kb.wait("v", SS)
            kb.op("v", lambda e: e.scalar_tensor_tensor(
                s_nmr, s_m, -1.0, s_r, ALU.mult, ALU.mult))
            kb.dinc("v", SV)
            for t in range(NT):
                rt = res[:, t * C:(t + 1) * C]
                xt = B1[:, t * C:(t + 1) * C]
                kb.op("v", lambda e, rt=rt, xt=xt, t=t: e.tensor_scalar(
                    xt, rt, s_r[:, t:t + 1], s_nmr[:, t:t + 1],
                    ALU.mult, ALU.add))
            kb.dinc("v", SV)
            kb.wait("t", SV)
            kb.tmode("trans")
            for t in range(NT):
                PT = PT0 if t % 2 == 0 else PT1
                if t >= 2:
                    kb.wait("t", SV)
                for ch in range(NCH):
                    src = B1[:, t * C + ch * 128: t * C + (ch + 1) * 128]
                    dst = PT[:, ch * 128:(ch + 1) * 128]
                    kb.op("t", lambda e, d=dst, s_=src: e.transpose(
                        d, s_, idb[:]), inc=(ST if ch == NCH - 1 else None))
                kb.wait("v", ST)
                src = bass.AP(PT, 0, [[1024, 128], [128, NCH], [1, 128]])
                dst = bass.AP(B2, t * 128, [[NT * C, 128], [T_LEN, NCH],
                                            [1, 128]])
                kb.op("v", lambda e, d=dst, s_=src: e.tensor_copy(d, s_))
                kb.dinc("v", SV)
            kb.wait("t", SV)

        def proj_wstat(w_hbm, n_ci, n_co, rhs_col0, evac_fn, w_rows,
                       base_col=0):
            slot = [WS[:, 0:1024], WS[:, 1024:2048]]
            slot_rows = w_rows // 128
            mm_done = {}
            dw_at = {}
            def slot3(i):
                return bass.AP(WS, i * 1024,
                               [[2048, 128], [128, slot_rows], [1, 128]])

            for co in range(min(2, n_co)):
                dma_g(slot3(co % 2), col_slices_hbm(w_hbm, co, w_rows,
                                                    base_col), guard=ST)
                dw_at[co] = DW.n
            kb.tmode("full")
            for co in range(n_co):
                kb.wait("t", DW, dw_at[co])
                sl = slot[co % 2]
                for qb in range(NQB):
                    bi, bank = next_bank()
                    for ci in range(n_ci):
                        lhsT = sl[:, ci * 128:(ci + 1) * 128]
                        rhs = rhs_col0(ci, qb)
                        kb.op("t", lambda e, b=bank, l=lhsT, r=rhs,
                              st=(ci == 0), sp=(ci == n_ci - 1): e.matmul(
                                  b, l, r, start=st, stop=sp,
                                  skip_group_check=True),
                              inc=(ST if ci == n_ci - 1 else None))
                    evac_fn(co, qb, bank, bi)
                mm_done[co] = ST.n
                nxt = co + 2
                if nxt < n_co:
                    kb.wait("g", ST, mm_done[co])
                    dma_g(slot3(nxt % 2),
                          col_slices_hbm(w_hbm, nxt, w_rows, base_col))
                    dw_at[nxt] = DW.n

        def evac_to_fm(dst_tensor, dst_base, eng, sem, bias_col=None,
                       func=AF.Identity):
            def f(co, qb, bank, bi):
                dst = dst_tensor[:, dst_base + co * T_LEN + qb * 512:
                                 dst_base + co * T_LEN + qb * 512 + 512]
                kb.wait(eng, ST)
                if eng == "s":
                    b = (bias_col[:, co:co + 1] if bias_col is not None
                         else s_zero)
                    kb.op("s", lambda e, d=dst, p=bank, b=b: e.activation(
                        d, p, func, bias=b, scale=1.0), inc=sem)
                else:
                    kb.op("v", lambda e, d=dst, p=bank: e.tensor_copy(d, p),
                          inc=sem)
                free_bank(bi, sem)
            return f

        def proj_v(lhs_col0, n_ci, rhs_res):
            kb.tmode("full")
            for qt in range(NT):
                for eh in range(2):
                    bi, bank = next_bank()
                    for ci in range(n_ci):
                        lhsT = lhs_col0(ci, qt)
                        rhs = rhs_res[:, ci * 1024 + eh * 512:
                                      ci * 1024 + eh * 512 + 512]
                        kb.op("t", lambda e, b=bank, l=lhsT, r=rhs,
                              st=(ci == 0), sp=(ci == n_ci - 1): e.matmul(
                                  b, l, r, start=st, stop=sp,
                                  skip_group_check=True),
                              inc=(ST if ci == n_ci - 1 else None))
                    kb.wait("v", ST)
                    dst = bass.AP(VT, (qt * 16 + eh * 8) * 65,
                                  [[VT_COLS, 128], [65, 8], [1, 64]])
                    src = bass.AP(bank.tensor, bank.offset,
                                  [[bank.tensor.shape[1], 128], [64, 8],
                                   [1, 64]])
                    kb.op("v", lambda e, d=dst, s_=src: e.tensor_copy(d, s_),
                          inc=SV)
                    free_bank(bi, SV)

        def attention(q_t, k_t, with_bias_row):
            kc_par = {"i": 0}
            ps_exp = [0, 0]
            ps_sv = [0, 0]
            av_free_sv = SV.n

            for qb in range(NQB):
                n_kc = 4 * (qb + 1)
                for hp in range(NHP):
                    for kc in range(n_kc):
                        buf = kc_par["i"] % 2
                        kc_par["i"] += 1
                        PS = PS0 if buf == 0 else PS1
                        diag = kc >= 4 * qb
                        off = (kc - 4 * qb) * 128 if diag else 0
                        w = 512 - off
                        kb.tmode("row64")
                        kb.wait("t", SS, ps_exp[buf])
                        if ps_sv[buf]:
                            kb.wait("t", SV, ps_sv[buf])
                            ps_sv[buf] = 0
                        for h2 in range(2):
                            rb = h2 * 64
                            kcol = hp * T_LEN + kc * 128
                            qcol = hp * T_LEN + qb * 512 + off
                            lhsT = k_t[rb:rb + 64, kcol:kcol + 128]
                            rhs = q_t[rb:rb + 64, qcol:qcol + w]
                            out = PS[:, h2 * 512 + off: h2 * 512 + off + w]
                            kb.op("t", lambda e, o=out, l=lhsT, r=rhs, rb=rb:
                                  e.matmul(o, l, r, start=True, stop=True,
                                           skip_group_check=True,
                                           tile_position=(rb, 0)),
                                  inc=(ST if h2 == 1 else None))
                        kb.wait("s", ST)
                        src = bass.AP(PS, off, [[1024, 128], [512, 2], [1, w]])
                        dst = bass.AP(pb[buf].tensor, pb[buf].offset + off,
                                      [[NT * C, 128], [512, 2], [1, w]])
                        kb.op("s", lambda e, d=dst, s_=src: e.activation(
                            d, s_, AF.Exp, scale=ATT_SCALE))
                        kb.dinc("s", SS)
                        ps_exp[buf] = SS.n
                        if diag:
                            kb.wait("v", SS)
                            mt = bass.AP(pb[buf].tensor, pb[buf].offset + off,
                                         [[NT * C, 128], [512, 2], [1, 128]])
                            mb = bass.AP(maskb, 0,
                                         [[128, 128], [0, 2], [1, 128]])
                            kb.op("v", lambda e, m=mt, mb=mb: e.tensor_tensor(
                                m, m, mb, ALU.mult))
                            kb.dinc("v", SV)
                            mask_sv = SV.n
                        if diag:
                            kb.wait("t", SV, mask_sv)
                        else:
                            kb.wait("t", SS)
                        if kc == 0 and av_free_sv:
                            kb.wait("t", SV, av_free_sv)
                        kb.tmode("full")
                        qt_lo = max(0, kc - 4 * qb)
                        for h2 in range(2):
                            h = 2 * hp + h2
                            bank = PA0 if h2 == 0 else PA1
                            for qtl in range(qt_lo, 4):
                                lhsT = pb[buf][:, h2 * 512 + qtl * 128:
                                               h2 * 512 + (qtl + 1) * 128]
                                rhs = bass.AP(VT, (kc * 16 + h) * 65,
                                              [[VT_COLS, 128], [1, 65]])
                                out = bank[:, qtl * 65: qtl * 65 + 65]
                                first = (kc == 0 and qtl == 0)
                                last = (kc == n_kc - 1 and qtl == 3 and
                                        h2 == 1)
                                kb.op("t", lambda e, o=out, l=lhsT, r=rhs,
                                      st=first: e.matmul(
                                          o, l, r, start=st, stop=False,
                                          skip_group_check=True),
                                      inc=(ST if last else None))
                    kb.wait("v", ST)
                    for h2 in range(2):
                        bank = PA0 if h2 == 0 else PA1
                        for qtl in range(4):
                            dcol = bank[:, qtl * 65 + 64: qtl * 65 + 65]
                            ix = h2 * 4 + qtl
                            kb.op("v", lambda e, d=dcol, ix=ix: e.reciprocal(
                                s_rden[:, ix:ix + 1], d))
                    kb.dinc("v", SV)
                    for h2 in range(2):
                        bank = PA0 if h2 == 0 else PA1
                        for qtl in range(4):
                            ix = h2 * 4 + qtl
                            src = bank[:, qtl * 65: qtl * 65 + 64]
                            dst = avn[h2][:, qtl * 64:(qtl + 1) * 64]
                            kb.op("v", lambda e, d=dst, s_=src, ix=ix:
                                  e.tensor_scalar(d, s_,
                                                  s_rden[:, ix:ix + 1],
                                                  None, ALU.mult))
                    kb.dinc("v", SV)
                    av_free_sv = SV.n
                    PT = PT0 if hp % 2 == 0 else PT1
                    kb.wait("t", SV)
                    kb.tmode("trans")
                    for h2 in range(2):
                        for qtl in range(4):
                            src = avn[h2][:, qtl * 64:(qtl + 1) * 64]
                            dst = PT[h2 * 64:h2 * 64 + 64,
                                     qtl * 128:(qtl + 1) * 128]
                            last = (h2 == 1 and qtl == 3)
                            kb.op("t", lambda e, d=dst, s_=src, h2=h2:
                                  e.transpose(d, s_, idb[:],
                                              tile_position=(0, h2 * 64)),
                                  inc=(ST if last else None))
                    kb.wait("v", ST)
                    dst = avfm[:, hp * 512:(hp + 1) * 512]
                    kb.op("v", lambda e, d=dst, s_=PT: e.tensor_copy(
                        d, s_[:, 0:512]))
                    kb.dinc("v", SV)
                kb.wait("t", SV)
                kb.tmode("full")
                for qtl in range(4):
                    qt_g = qb * 4 + qtl
                    for ob in range(2):
                        bi, bank = next_bank()
                        for ch in range(NCH):
                            lhsT = avfm[:, ch * 512 + qtl * 128:
                                        ch * 512 + (qtl + 1) * 128]
                            rhs = wo_res[:, ch * 1024 + ob * 512:
                                         ch * 1024 + ob * 512 + 512]
                            last_mm = (ch == NCH - 1) and not with_bias_row
                            kb.op("t", lambda e, b=bank, l=lhsT, r=rhs,
                                  st=(ch == 0), sp=last_mm: e.matmul(
                                      b, l, r, start=st, stop=sp,
                                      skip_group_check=True),
                                  inc=(ST if last_mm else None))
                        if with_bias_row:
                            rhs = bias_row[:, ob * 512:(ob + 1) * 512]
                            kb.op("t", lambda e, b=bank, l=c128[:], r=rhs:
                                  e.matmul(b, l, r, start=False, stop=True,
                                           skip_group_check=True), inc=ST)
                        kb.wait("v", ST)
                        dstr = res[:, qt_g * C + ob * 512:
                                   qt_g * C + ob * 512 + 512]
                        kb.op("v", lambda e, d=dstr, p=bank: e.tensor_tensor(
                            d, d, p, ALU.add), inc=SV)
                        free_bank(bi, SV)
                ps_sv[0] = SV.n
                ps_sv[1] = SV.n
            kb.dinc("v", SV)

        def attn_a():
            ln_and_transpose(first=(sub_idx[0] == 0))
            if debug:
                kb.wait("y", SV)
                dma_y(dbg_st[:], stats[:])
                dma_y(dbg_xtm[:], B1[:])

            def xnT_rhs(ci, qb):
                return B2[:, ci * T_LEN + qb * 512: ci * T_LEN + qb * 512 + 512]

            def xnT_lhs(ci, qt):
                return B2[:, ci * T_LEN + qt * 128: ci * T_LEN + qt * 128 + 128]

            kb.wait("v", ST)
            kb.op("v", lambda e: e.memset(vt_ones, 1.0))
            kb.dinc("v", SV)
            proj_wstat(wkT, NCH, NCH, xnT_rhs,
                       evac_to_fm(KF, 0, "s", SS), C)
            kb.dinc("s", SS)
            dma_g(B1.reshape([128, 16, 1024])[:, 0:8, :], whole_hbm(wvT),
                  guard=ST)
            wv_dw = DW.n
            kb.wait("t", DW, wv_dw)
            proj_v(xnT_lhs, NCH, B1[:, 0:8192])
            proj_wstat(wqT, NCH, NCH, xnT_rhs,
                       evac_to_fm(B1, 0, "v", SV), C)
            kb.dinc("v", SV)
            dst = bass.AP(B2, 6656, [[NT * C, 128], [1024, 8], [1, 1024]])
            dma_g(dst, whole_hbm(woT), guard=ST)
            kb.wait("t", DW)
            kb.wait("t", SV)
            kb.wait("t", SS)
            if debug:
                kb.wait("y", SV)
                kb.wait("y", SS)
                dma_y(dbg_xn[:], B2[:])
                dma_y(dbg_q[:], B1[:])
                dma_y(dbg_k[:], KF[:])
                dma_y(dbg_vt[:], VT[:])
            attention(B1, KF, with_bias_row=False)

        def ffn():
            ln_and_transpose(first=False)

            def xnT_lhs(ci, qt):
                return B2[:, ci * T_LEN + qt * 128: ci * T_LEN + qt * 128 + 128]

            if not ffn_state["q"]:
                dma_y(bass.AP(KFf, 0, [[8192, 128], [1024, 8], [1, 1024]]),
                      whole_hbm(fwT), guard=ST)
                kb.wait("v", DY)
                kb.wait("v", ST)
                w_all = KFf[:, 0:8192]
                kb.op("v", lambda e: e.tensor_reduce(
                    s_tot, w_all, AXL.X, ALU.add, apply_absolute_value=True))
                kb.dinc("v", SV)
                kb.wait("t", SV)
                kb.tmode("fp32a")
                kb.op("t", lambda e: e.matmul(
                    PA0[0:1, 0:1], onesf[:, 0:1], s_tot, start=True,
                    stop=True, skip_group_check=True), inc=ST)
                kb.wait("v", ST)
                kb.op("v", lambda e: e.tensor_copy(s_msq[0:1, 0:1],
                                                   PA0[0:1, 0:1]))
                kb.dinc("v", SV)
                kb.wait("t", SV)
                kb.tmode("fp32b")
                kb.op("t", lambda e: e.matmul(
                    PA1[:, 0:1], onesf[0:1, :], s_msq[0:1, 0:1], start=True,
                    stop=True, skip_group_check=True), inc=ST)
                kb.wait("v", ST)
                kb.op("v", lambda e: e.tensor_scalar(
                    s_scale, PA1[:, 0:1], 1.0 / (C * C), 1e-5,
                    ALU.mult, ALU.max))
                kb.dinc("v", SV)
                kb.op("v", lambda e: e.reciprocal(s_rscale, s_scale))
                kb.dinc("v", SV)
                u = VTf[:, 0:8192]
                kb.op("v", lambda e: e.tensor_scalar(
                    u, w_all, s_rscale[:, 0:1], None, ALU.mult))
                kb.dinc("v", SV)
                g_ = B1f[:, 0:8192]
                l_ = KFf[:, 0:8192]
                kb.op("v", lambda e: e.tensor_scalar(g_, u, 0.5, None,
                                                     ALU.is_gt))
                kb.op("v", lambda e: e.tensor_scalar(l_, u, -0.5, None,
                                                     ALU.is_lt))
                kb.dinc("v", SV)
                kb.op("v", lambda e: e.tensor_tensor(
                    VT8[:, 0:8192], g_, l_, ALU.subtract))
                kb.dinc("v", SV)
                kb.wait("y", SV)
                dma_y(whole_hbm(wt8),
                      bass.AP(VT8, 0, [[2 * VT_COLS, 128], [1024, 8],
                                       [1, 1024]]))
                ffn_state["q"] = True
                ffn_state["store_dy"] = DY.n
            else:
                dma_y(bass.AP(VT8, 0, [[2 * VT_COLS, 128], [1024, 8],
                                       [1, 1024]]),
                      whole_hbm(wt8), guard=ST)
            wt_dy = DY.n
            kb.wait("t", DY, wt_dy)
            for qt in range(NT):
                for ob in range(2):
                    bi, bank = next_bank()
                    for ci in range(NCH):
                        lhsT = xnT_lhs(ci, qt)
                        rhs = VT8[:, ci * 1024 + ob * 512:
                                  ci * 1024 + ob * 512 + 512]
                        kb.op("t", lambda e, b=bank, l=lhsT, r=rhs,
                              st=(ci == 0), sp=(ci == NCH - 1): e.matmul(
                                  b, l, r, start=st, stop=sp,
                                  skip_group_check=True),
                              inc=(ST if ci == NCH - 1 else None))
                    kb.wait("v", ST)
                    dstr = res[:, qt * C + ob * 512: qt * C + ob * 512 + 512]
                    kb.op("v", lambda e, d=dstr, p=bank:
                          e.scalar_tensor_tensor(d, p, s_scale[:, 0:1], d,
                                                 ALU.mult, ALU.add), inc=SV)
                    free_bank(bi, SV)
            kb.dinc("v", SV)
            if ffn_state.get("store_dy"):
                kb.wait("v", DY, ffn_state["store_dy"])
                ffn_state["store_dy"] = 0

        def attn_b():
            ln_and_transpose(first=False)

            def xnT_rhs(ci, qb):
                return B2[:, ci * T_LEN + qb * 512: ci * T_LEN + qb * 512 + 512]

            kb.wait("v", ST)
            kb.op("v", lambda e: e.memset(vt_ones, 1.0))
            kb.dinc("v", SV)
            proj_wstat(akd, NCH, 4, xnT_rhs,
                       evac_to_fm(B1, 0, "s", SS, bias_col=bkd_t,
                                  func=AF.Gelu), C)
            kb.dinc("s", SS)
            kb.wait("v", SS)
            for lc in range(2):
                dst = B1[:, (4 + lc) * T_LEN:(5 + lc) * T_LEN]
                a_ = B1[:, lc * T_LEN:(lc + 1) * T_LEN]
                b_ = B1[:, (2 + lc) * T_LEN:(3 + lc) * T_LEN]
                kb.op("v", lambda e, d=dst, a=a_, b=b_: e.tensor_tensor(
                    d, a, b, ALU.add))
            kb.dinc("v", SV)

            def lat_rhs(ci, qb):
                base = (4 + ci) * T_LEN + qb * 512
                return B1[:, base:base + 512]

            def lat_lhs(ci, qt):
                base = (4 + ci) * T_LEN + qt * 128
                return B1[:, base:base + 128]

            kb.wait("t", SV)
            proj_wstat(aku, 2, NCH, lat_rhs,
                       evac_to_fm(KF, 0, "s", SS, bias_col=bkuk_t,
                                  func=AF.Identity), LAT, base_col=0)
            kb.dinc("s", SS)
            dma_g(B1.reshape([128, 16, 1024])[:, 0:2, :],
                  aku[:, C:2 * C].rearrange("(ci p) co -> p ci co", p=128),
                  guard=SV)
            kb.wait("t", DW)
            kb.wait("t", SS)
            proj_v(lat_lhs, 2, B1[:, 0:2048])
            proj_wstat(aq, NCH, NCH, xnT_rhs,
                       evac_to_fm(B1, 0, "s", SS, bias_col=bq_t,
                                  func=AF.Identity), C)
            kb.dinc("s", SS)
            dst = bass.AP(B2, 6656, [[NT * C, 128], [1024, 8], [1, 1024]])
            dma_g(dst, whole_hbm(awo), guard=ST)
            kb.wait("t", DW)
            kb.tmode("full")
            for ob in range(2):
                bank = PA0[:, 0:512] if ob == 0 else PA1[:, 0:512]
                for ci in range(NCH):
                    lhsT = foldt[:, ci * 128:(ci + 1) * 128]
                    rhs = wo_res[:, ci * 1024 + ob * 512:
                                 ci * 1024 + ob * 512 + 512]
                    kb.op("t", lambda e, b=bank, l=lhsT, r=rhs, st=(ci == 0),
                          sp=(ci == NCH - 1): e.matmul(
                              b, l, r, start=st, stop=sp,
                              skip_group_check=True),
                          inc=(ST if ci == NCH - 1 else None))
                kb.wait("v", ST)
                dst2 = bias_row[:, ob * 512:(ob + 1) * 512]
                kb.op("v", lambda e, d=dst2, p=bank, ob=ob: e.tensor_tensor(
                    d, p, bo_bf[:, ob * 512:(ob + 1) * 512], ALU.add))
            kb.dinc("v", SV)
            kb.wait("t", SV)
            kb.wait("t", SS)
            attention(B1, KF, with_bias_row=True)

        ffn_state = {"q": False, "store_dy": 0}
        sub_idx = [0]
        phases = [attn_a, ffn, attn_b, ffn]
        for seq in range(2):
            if sub_idx[0] >= n_sublayers:
                break
            if seq == 1:
                kb.wait("y", SV)
                DO.n += 16
                kb.emit("y", lambda e: e.dma_start(
                    y0.rearrange("(t p) c -> p t c", p=128),
                    res3).then_inc(DO.h, 16))
                kb.wait("y", DO)
                dma_y(res3,
                      x1.rearrange("(t p) c -> p t c", p=128))
                xdy = DY.n
                kb.wait("v", DY, xdy)
                kb.wait("s", DY, xdy)
            for ph in phases:
                if sub_idx[0] >= n_sublayers:
                    break
                ph()
                sub_idx[0] += 1
        ran_seq1 = sub_idx[0] > 4
        kb.wait("y", SV)
        if ran_seq1:
            DO.n += 16
            kb.emit("y", lambda e: e.dma_start(
                y1.rearrange("(t p) c -> p t c", p=128),
                res3).then_inc(DO.h, 16))
        else:
            DO.n += 16
            kb.emit("y", lambda e: e.dma_start(
                y0.rearrange("(t p) c -> p t c", p=128),
                res3).then_inc(DO.h, 16))
            DO.n += 16
            kb.emit("y", lambda e: e.dma_start(
                y1.rearrange("(t p) c -> p t c", p=128),
                res3).then_inc(DO.h, 16))
        kb.wait("y", DO)

        with nc.Block() as block:
            @block.tensor
            def _(t):
                for f in kb.ops["t"]:
                    f(t)

            @block.vector
            def _(v):
                for f in kb.ops["v"]:
                    f(v)

            @block.scalar
            def _(s):
                for f in kb.ops["s"]:
                    f(s)

            @block.gpsimd
            def _(g):
                for f in kb.ops["g"]:
                    f(g)

            @block.sync
            def _(y):
                for f in kb.ops["y"]:
                    f(y)

    return nc


def _shared_inputs(inputs):
    d = inputs

    def a(x):
        return np.ascontiguousarray(np.asarray(x, dtype=np.float32))

    return {
        "wqT": a(np.asarray(d["L0_attn_a_wq"]).T),
        "wkT": a(np.asarray(d["L0_attn_a_wk"]).T),
        "wvT": a(np.asarray(d["L0_attn_a_wv"]).T),
        "woT": a(np.asarray(d["L0_attn_a_wo"]).T),
        "aq": a(d["L0_attn_b_Wq"]),
        "akd": a(d["L0_attn_b_Wkd"]),
        "aku": a(d["L0_attn_b_Wku"]),
        "awo": a(d["L0_attn_b_Wo"]),
        "fwT": a(np.asarray(d["L0_ffn_weight"]).T),
        "bq": a(np.asarray(d["L0_attn_b_bq"]).reshape(8, 128).T),
        "bkd": a(np.asarray(d["L0_attn_b_bkd"]).reshape(4, 128).T),
        "bkuk": a(np.asarray(d["L0_attn_b_bku"])[:1024].reshape(8, 128).T),
        "bkuv": a(np.broadcast_to(
            np.asarray(d["L0_attn_b_bku"])[1024:].reshape(8, 128).T[:, :, None],
            (128, 8, 128)).reshape(128, 1024)),
        "bo": a(np.broadcast_to(
            np.asarray(d["L0_attn_b_bo"])[None, :], (128, 1024))),
    }


def kernel(**inputs):
    nc = build_program()
    sh = _shared_inputs(inputs)
    x = np.asarray(inputs["L0_attn_a_x"], dtype=np.float32)
    in_maps = []
    for core in range(8):
        m = dict(sh)
        m["x0"] = np.ascontiguousarray(x[2 * core])
        m["x1"] = np.ascontiguousarray(x[2 * core + 1])
        in_maps.append(m)
    res = run_bass_kernel_spmd(nc, in_maps, core_ids=list(range(8)))
    out = np.empty((16, T_LEN, C), dtype=np.float32)
    for core in range(8):
        out[2 * core] = res.results[core]["y0"]
        out[2 * core + 1] = res.results[core]["y1"]
    return out


# version 04
# speedup vs baseline: 1.5860x; 1.3903x over previous
import numpy as np
import concourse.bass as bass
import concourse.mybir as mybir
import bass_rust as _br
from concourse.bass_utils import run_bass_kernel_spmd
from contextlib import ExitStack

F32 = mybir.dt.float32
BF16 = mybir.dt.bfloat16
FP8 = mybir.dt.float8e4
I32 = mybir.dt.int32
AF = mybir.ActivationFunctionType
ALU = mybir.AluOpType
AXL = mybir.AxisListType

T_LEN = 2048
C = 1024
NT = 16
NCH = 8
NQB = 4
NHP = 8
LAT = 256
EPS = 1e-5
ATT_SCALE = 0.125
VT_COLS = 16 * 16 * 65


class Ctr:
    def __init__(self, h):
        self.h = h
        self.n = 0


class KB:
    ENGS = ("t", "v", "s", "g", "y")

    def __init__(self):
        self.ops = {e: [] for e in self.ENGS}
        self.tmode_cur = [None]

    def emit(self, eng, fn):
        self.ops[eng].append(fn)

    def wait(self, eng, ctr, val=None):
        v = ctr.n if val is None else val
        if v <= 0:
            return
        h = ctr.h
        self.emit(eng, lambda e: e.wait_ge(h, v))

    def op(self, eng, fn, inc=None):
        if inc is None:
            self.emit(eng, fn)
        elif eng == "t":
            inc.n += 1
            h = inc.h
            self.emit(eng, lambda e: fn(e).then_inc(h, 1))
        else:
            inc.n += 1
            h = inc.h
            self.emit(eng, fn)
            self.emit(eng, lambda e: e.drain(fusable=False).then_inc(h, 1))

    def dinc(self, eng, ctr):
        ctr.n += 1
        h = ctr.h
        self.emit(eng, lambda e: e.drain(fusable=False).then_inc(h, 1))

    TMODE_DRAINS = False
    T_DRAIN_INCS = False

    def tmode(self, m):
        if self.tmode_cur[0] != m:
            if self.TMODE_DRAINS and self.tmode_cur[0] is not None:
                self.emit("t", lambda e: e.drain(fusable=False))
            self.tmode_cur[0] = m


def build_program(n_sublayers=8, debug=False):
    nc = bass.Bass("TRN2", target_bir_lowering=False)

    x0 = nc.dram_tensor("x0", [T_LEN, C], F32, kind="ExternalInput")
    x1 = nc.dram_tensor("x1", [T_LEN, C], F32, kind="ExternalInput")
    wqT = nc.dram_tensor("wqT", [C, C], F32, kind="ExternalInput")
    wkT = nc.dram_tensor("wkT", [C, C], F32, kind="ExternalInput")
    wvT = nc.dram_tensor("wvT", [C, C], F32, kind="ExternalInput")
    woT = nc.dram_tensor("woT", [C, C], F32, kind="ExternalInput")
    aq = nc.dram_tensor("aq", [C, C], F32, kind="ExternalInput")
    akd = nc.dram_tensor("akd", [C, 512], F32, kind="ExternalInput")
    aku = nc.dram_tensor("aku", [LAT, 2 * C], F32, kind="ExternalInput")
    awo = nc.dram_tensor("awo", [C, C], F32, kind="ExternalInput")
    fwT = nc.dram_tensor("fwT", [C, C], F32, kind="ExternalInput")
    bq_h = nc.dram_tensor("bq", [128, 8], F32, kind="ExternalInput")
    bkd_h = nc.dram_tensor("bkd", [128, 4], F32, kind="ExternalInput")
    bkuk_h = nc.dram_tensor("bkuk", [128, 8], F32, kind="ExternalInput")
    bkuv_h = nc.dram_tensor("bkuv", [128, 1024], F32, kind="ExternalInput")
    bo_h = nc.dram_tensor("bo", [128, C], F32, kind="ExternalInput")
    if debug:
        dbg_st = nc.dram_tensor("dbg_st", [128, 128], F32,
                                kind="ExternalOutput")
        dbg_xtm = nc.dram_tensor("dbg_xtm", [128, NT * C], BF16,
                                 kind="ExternalOutput")
        dbg_xn = nc.dram_tensor("dbg_xn", [128, NT * C], BF16,
                                kind="ExternalOutput")
        dbg_q = nc.dram_tensor("dbg_q", [128, NT * C], BF16,
                               kind="ExternalOutput")
        dbg_k = nc.dram_tensor("dbg_k", [128, NT * C], BF16,
                               kind="ExternalOutput")
        dbg_vt = nc.dram_tensor("dbg_vt", [128, VT_COLS], BF16,
                                kind="ExternalOutput")
    y0 = nc.dram_tensor("y0", [T_LEN, C], F32, kind="ExternalOutput")
    y1 = nc.dram_tensor("y1", [T_LEN, C], F32, kind="ExternalOutput")
    wt8 = nc.dram_tensor("wt8", [C, C], FP8)

    with ExitStack() as ctx:
        en = ctx.enter_context
        DW = Ctr(en(nc.semaphore("DW")))
        DY = Ctr(en(nc.semaphore("DY")))
        DO = Ctr(en(nc.semaphore("DO")))
        ST = Ctr(en(nc.semaphore("ST")))
        SV = Ctr(en(nc.semaphore("SV")))
        SS = Ctr(en(nc.semaphore("SS")))
        SG = Ctr(en(nc.semaphore("SG")))

        res = en(nc.sbuf_tensor("res", [128, NT * C], F32))
        B1 = en(nc.sbuf_tensor("B1", [128, NT * C], BF16))
        B2 = en(nc.sbuf_tensor("B2", [128, NT * C], BF16))
        KF = en(nc.sbuf_tensor("KF", [128, NT * C], BF16))
        VT = en(nc.sbuf_tensor("VT", [128, VT_COLS], BF16))
        WS = en(nc.sbuf_tensor("WS", [128, 2 * 1024], BF16))
        idb = en(nc.sbuf_tensor("idb", [128, 128], BF16))
        maskb = en(nc.sbuf_tensor("maskb", [128, 128], BF16))
        onesr = en(nc.sbuf_tensor("onesr", [128, 128], BF16))
        stats = en(nc.sbuf_tensor("stats", [128, 128], F32))
        bo_bf = en(nc.sbuf_tensor("bo_bf", [128, 1024], BF16))
        foldt = en(nc.sbuf_tensor("foldt", [128, 1024], BF16))
        c128 = en(nc.sbuf_tensor("c128", [128, 128], BF16))
        bq_t = en(nc.sbuf_tensor("bq_t", [128, 8], F32))
        bkd_t = en(nc.sbuf_tensor("bkd_t", [128, 4], F32))
        bkuk_t = en(nc.sbuf_tensor("bkuk_t", [128, 8], F32))
        onesf = en(nc.sbuf_tensor("onesf", [128, 128], F32))

        PS0 = en(nc.psum_tensor("PS0", [128, 1024], F32))
        PS1 = en(nc.psum_tensor("PS1", [128, 1024], F32))
        PA0 = en(nc.psum_tensor("PA0", [128, 512], F32))
        PA1 = en(nc.psum_tensor("PA1", [128, 512], F32))
        PT0 = en(nc.psum_tensor("PT0", [128, 1024], BF16))
        PT1 = en(nc.psum_tensor("PT1", [128, 1024], BF16))

        s_sumx = stats[:, 0:16]
        s_sumxx = stats[:, 16:32]
        s_m = stats[:, 32:48]
        s_var = stats[:, 48:64]
        s_r = stats[:, 64:80]
        s_nmr = stats[:, 80:96]
        s_eps = stats[:, 96:97]
        s_scale = stats[:, 97:98]
        s_rscale = stats[:, 98:99]
        s_tot = stats[:, 99:100]
        s_rden = stats[:, 100:108]
        s_msq = stats[:, 108:124]
        s_zero = stats[:, 124:125]

        B1f = B1.bitcast(F32)
        KFf = KF.bitcast(F32)
        VTf = VT.bitcast(F32)
        VT8 = VT.bitcast(FP8)
        B1i = B1.bitcast(I32)

        pb = [B2[:, 0:1024], B2[:, 1024:2048]]
        avfm = B2[:, 2048:6144]
        avn = [B2[:, 6144:6400], B2[:, 6400:6656]]
        wo_res = B2[:, 6656:14848]
        bias_row = B2[:, 14848:15872]
        scr1 = B2[:, 0:1024]
        scr2 = B2[:, 1024:2048]

        kb = KB()
        res3 = bass.AP(res, 0, [[NT * C, 128], [C, NT], [1, C]])

        def dma_g(dst_ap, src_ap, guard=None):
            if guard is not None:
                kb.wait("g", guard)
            DW.n += 16
            kb.emit("g", lambda e: e.dma_start(dst_ap, src_ap).then_inc(DW.h, 16))

        def dma_y(dst_ap, src_ap, guard=None):
            if guard is not None:
                kb.wait("y", guard)
            DY.n += 16
            kb.emit("y", lambda e: e.dma_start(dst_ap, src_ap).then_inc(DY.h, 16))

        def col_slices_hbm(w2d, co, rows, base_col=0):
            a = base_col + co * 128
            return w2d[:, a:a + 128].rearrange("(ci p) co -> p ci co", p=128)

        def whole_hbm(w2d):
            return w2d.rearrange("(ci p) co -> p ci co", p=128)

        gen_banks = [PS0[:, 0:512], PS0[:, 512:1024],
                     PS1[:, 0:512], PS1[:, 512:1024]]
        gb = {"i": 0, "free": [(0, None)] * 4}

        def next_bank():
            i = gb["i"] % 4
            gb["i"] += 1
            v, ctr = gb["free"][i]
            if ctr is not None:
                kb.wait("t", ctr, v)
            return i, gen_banks[i]

        def free_bank(i, ctr):
            gb["free"][i] = (ctr.n, ctr)

        kb.op("g", lambda e: e.iota(B1i[:, 0:128], [[-1, 128]], base=0,
                                    channel_multiplier=1), inc=SG)
        kb.wait("v", SG)
        kb.op("v", lambda e: e.tensor_scalar(idb[:], B1i[:, 0:128], 0.0, None,
                                             ALU.is_equal))
        kb.op("v", lambda e: e.tensor_scalar(maskb[:], B1i[:, 0:128], 0.0, None,
                                             ALU.is_le))
        kb.op("v", lambda e: e.memset(onesr[0:1, :], 1.0))
        kb.op("v", lambda e: e.memset(s_eps, EPS))
        kb.op("v", lambda e: e.memset(s_zero, 0.0))
        kb.op("v", lambda e: e.memset(onesf[:], 1.0))
        kb.op("v", lambda e: e.memset(c128[:], 1.0 / 128.0))
        vt_ones = bass.AP(VT, 64, [[VT_COLS, 128], [65, 256]])
        kb.op("v", lambda e: e.memset(vt_ones, 1.0))
        kb.dinc("v", SV)
        dma_y(bq_t[:], bq_h[:])
        dma_y(bkd_t[:], bkd_h[:])
        dma_y(bkuk_t[:], bkuk_h[:])
        dma_g(foldt[:], bkuv_h[:])
        dma_g(bo_bf[:], bo_h[:])
        dma_y(res3, x0.rearrange("(t p) c -> p t c", p=128))
        setup_dy = DY.n
        setup_dw = DW.n

        def ln_and_transpose(first):
            kb.wait("s", ST)
            kb.wait("v", ST)
            if first:
                kb.wait("v", DY, setup_dy)
                kb.wait("s", DY, setup_dy)
                kb.wait("t", DW, setup_dw)
            kb.wait("s", SV)
            for t in range(NT):
                rt = res[:, t * C:(t + 1) * C]
                kb.op("v", lambda e, rt=rt, t=t: e.tensor_scalar(
                    scr1[:], rt, 1.0, None, ALU.mult, op1=ALU.add,
                    accum_out=s_sumx[:, t:t + 1]))
                kb.op("s", lambda e, rt=rt, t=t: e.activation(
                    scr2[:], rt, AF.Square, accum_out=s_sumxx[:, t:t + 1]))
            kb.dinc("v", SV)
            kb.dinc("s", SS)
            kb.wait("v", SS)
            kb.op("v", lambda e: e.tensor_scalar(s_m, s_sumx, 1.0 / C, None,
                                                 ALU.mult))
            kb.dinc("v", SV)
            kb.op("v", lambda e: e.tensor_tensor(s_msq, s_m, s_m, ALU.mult))
            kb.dinc("v", SV)
            kb.op("v", lambda e: e.scalar_tensor_tensor(
                s_var, s_sumxx, 1.0 / C, s_msq, ALU.mult, ALU.subtract))
            kb.dinc("v", SV)
            kb.wait("s", SV)
            kb.op("s", lambda e: e.activation(s_r, s_var, AF.Ln,
                                              bias=s_eps, scale=1.0))
            kb.dinc("s", SS)
            kb.op("s", lambda e: e.activation(s_r, s_r, AF.Exp, scale=-0.5))
            kb.dinc("s", SS)
            kb.wait("v", SS)
            kb.op("v", lambda e: e.scalar_tensor_tensor(
                s_nmr, s_m, -1.0, s_r, ALU.mult, ALU.mult))
            kb.dinc("v", SV)
            for t in range(NT):
                rt = res[:, t * C:(t + 1) * C]
                xt = B1[:, t * C:(t + 1) * C]
                kb.op("v", lambda e, rt=rt, xt=xt, t=t: e.tensor_scalar(
                    xt, rt, s_r[:, t:t + 1], s_nmr[:, t:t + 1],
                    ALU.mult, ALU.add))
            kb.dinc("v", SV)
            kb.wait("t", SV)
            kb.tmode("trans")
            for t in range(NT):
                PT = PT0 if t % 2 == 0 else PT1
                if t >= 2:
                    kb.wait("t", SV)
                for ch in range(NCH):
                    src = B1[:, t * C + ch * 128: t * C + (ch + 1) * 128]
                    dst = PT[:, ch * 128:(ch + 1) * 128]
                    kb.op("t", lambda e, d=dst, s_=src: e.transpose(
                        d, s_, idb[:]), inc=(ST if ch == NCH - 1 else None))
                kb.wait("v", ST)
                src = bass.AP(PT, 0, [[1024, 128], [128, NCH], [1, 128]])
                dst = bass.AP(B2, t * 128, [[NT * C, 128], [T_LEN, NCH],
                                            [1, 128]])
                kb.op("v", lambda e, d=dst, s_=src: e.tensor_copy(d, s_))
                kb.dinc("v", SV)
            kb.wait("t", SV)

        def proj_wstat(w_hbm, n_ci, n_co, rhs_col0, evac_fn, w_rows,
                       base_col=0):
            slot = [WS[:, 0:1024], WS[:, 1024:2048]]
            slot_rows = w_rows // 128
            mm_done = {}
            dw_at = {}
            def slot3(i):
                return bass.AP(WS, i * 1024,
                               [[2048, 128], [128, slot_rows], [1, 128]])

            for co in range(min(2, n_co)):
                dma_g(slot3(co % 2), col_slices_hbm(w_hbm, co, w_rows,
                                                    base_col), guard=ST)
                dw_at[co] = DW.n
            kb.tmode("full")
            for co in range(n_co):
                kb.wait("t", DW, dw_at[co])
                sl = slot[co % 2]
                for qb in range(NQB):
                    bi, bank = next_bank()
                    for ci in range(n_ci):
                        lhsT = sl[:, ci * 128:(ci + 1) * 128]
                        rhs = rhs_col0(ci, qb)
                        kb.op("t", lambda e, b=bank, l=lhsT, r=rhs,
                              st=(ci == 0), sp=(ci == n_ci - 1): e.matmul(
                                  b, l, r, start=st, stop=sp,
                                  skip_group_check=True),
                              inc=(ST if ci == n_ci - 1 else None))
                    evac_fn(co, qb, bank, bi)
                mm_done[co] = ST.n
                nxt = co + 2
                if nxt < n_co:
                    kb.wait("g", ST, mm_done[co])
                    dma_g(slot3(nxt % 2),
                          col_slices_hbm(w_hbm, nxt, w_rows, base_col))
                    dw_at[nxt] = DW.n

        def evac_to_fm(dst_tensor, dst_base, eng, sem, bias_col=None,
                       func=AF.Identity):
            def f(co, qb, bank, bi):
                dst = dst_tensor[:, dst_base + co * T_LEN + qb * 512:
                                 dst_base + co * T_LEN + qb * 512 + 512]
                kb.wait(eng, ST)
                if eng == "s":
                    b = (bias_col[:, co:co + 1] if bias_col is not None
                         else s_zero)
                    kb.op("s", lambda e, d=dst, p=bank, b=b: e.activation(
                        d, p, func, bias=b, scale=1.0), inc=sem)
                else:
                    kb.op("v", lambda e, d=dst, p=bank: e.tensor_copy(d, p),
                          inc=sem)
                free_bank(bi, sem)
            return f

        def proj_v(lhs_col0, n_ci, rhs_res):
            kb.tmode("full")
            for qt in range(NT):
                for eh in range(2):
                    bi, bank = next_bank()
                    for ci in range(n_ci):
                        lhsT = lhs_col0(ci, qt)
                        rhs = rhs_res[:, ci * 1024 + eh * 512:
                                      ci * 1024 + eh * 512 + 512]
                        kb.op("t", lambda e, b=bank, l=lhsT, r=rhs,
                              st=(ci == 0), sp=(ci == n_ci - 1): e.matmul(
                                  b, l, r, start=st, stop=sp,
                                  skip_group_check=True),
                              inc=(ST if ci == n_ci - 1 else None))
                    kb.wait("v", ST)
                    dst = bass.AP(VT, (qt * 16 + eh * 8) * 65,
                                  [[VT_COLS, 128], [65, 8], [1, 64]])
                    src = bass.AP(bank.tensor, bank.offset,
                                  [[bank.tensor.shape[1], 128], [64, 8],
                                   [1, 64]])
                    kb.op("v", lambda e, d=dst, s_=src: e.tensor_copy(d, s_),
                          inc=SV)
                    free_bank(bi, SV)

        def attention(q_t, k_t, with_bias_row):
            kc_par = {"i": 0}
            ps_exp = [0, 0]
            ps_sv = [0, 0]
            av_free_sv = SV.n

            for qb in range(NQB):
                n_kc = 4 * (qb + 1)
                items = [(hp, kc) for hp in range(NHP) for kc in range(n_kc)]
                sc = {}

                def emit_scores(i, qb=qb, n_kc=n_kc, items=items, sc=sc):
                    hp, kc = items[i]
                    buf = kc_par["i"] % 2
                    kc_par["i"] += 1
                    PS = PS0 if buf == 0 else PS1
                    diag = kc >= 4 * qb
                    off = (kc - 4 * qb) * 128 if diag else 0
                    w = 512 - off
                    kb.tmode("row64")
                    kb.wait("t", SS, ps_exp[buf])
                    if ps_sv[buf]:
                        kb.wait("t", SV, ps_sv[buf])
                        ps_sv[buf] = 0
                    for h2 in range(2):
                        rb = h2 * 64
                        kcol = hp * T_LEN + kc * 128
                        qcol = hp * T_LEN + qb * 512 + off
                        lhsT = k_t[rb:rb + 64, kcol:kcol + 128]
                        rhs = q_t[rb:rb + 64, qcol:qcol + w]
                        out = PS[:, h2 * 512 + off: h2 * 512 + off + w]
                        kb.op("t", lambda e, o=out, l=lhsT, r=rhs, rb=rb:
                              e.matmul(o, l, r, start=True, stop=True,
                                       skip_group_check=True,
                                       tile_position=(rb, 0)),
                              inc=(ST if h2 == 1 else None))
                    sc[i] = (buf, off, w, ST.n, PS)

                emit_scores(0)
                for i, (hp, kc) in enumerate(items):
                    if i + 1 < len(items):
                        emit_scores(i + 1)
                    buf, off, w, st_sc, PS = sc.pop(i)
                    diag = kc >= 4 * qb
                    kb.wait("s", ST, st_sc)
                    src = bass.AP(PS, off, [[1024, 128], [512, 2], [1, w]])
                    dst = bass.AP(pb[buf].tensor, pb[buf].offset + off,
                                  [[NT * C, 128], [512, 2], [1, w]])
                    kb.op("s", lambda e, d=dst, s_=src: e.activation(
                        d, s_, AF.Exp, scale=ATT_SCALE))
                    kb.dinc("s", SS)
                    ps_exp[buf] = SS.n
                    if diag:
                        kb.wait("v", SS)
                        mt = bass.AP(pb[buf].tensor, pb[buf].offset + off,
                                     [[NT * C, 128], [512, 2], [1, 128]])
                        mb = bass.AP(maskb, 0,
                                     [[128, 128], [0, 2], [1, 128]])
                        kb.op("v", lambda e, m=mt, mb=mb: e.tensor_tensor(
                            m, m, mb, ALU.mult))
                        kb.dinc("v", SV)
                        mask_sv = SV.n
                    if diag:
                        kb.wait("t", SV, mask_sv)
                    else:
                        kb.wait("t", SS)
                    if kc == 0 and av_free_sv:
                        kb.wait("t", SV, av_free_sv)
                    kb.tmode("full")
                    qt_lo = max(0, kc - 4 * qb)
                    for h2 in range(2):
                        h = 2 * hp + h2
                        bank = PA0 if h2 == 0 else PA1
                        for qtl in range(qt_lo, 4):
                            lhsT = pb[buf][:, h2 * 512 + qtl * 128:
                                           h2 * 512 + (qtl + 1) * 128]
                            rhs = bass.AP(VT, (kc * 16 + h) * 65,
                                          [[VT_COLS, 128], [1, 65]])
                            out = bank[:, qtl * 65: qtl * 65 + 65]
                            first = (kc == 0 and qtl == 0)
                            last = (kc == n_kc - 1 and qtl == 3 and h2 == 1)
                            kb.op("t", lambda e, o=out, l=lhsT, r=rhs,
                                  st=first: e.matmul(
                                      o, l, r, start=st, stop=False,
                                      skip_group_check=True),
                                  inc=(ST if last else None))
                    if kc != n_kc - 1:
                        continue
                    kb.wait("v", ST)
                    for h2 in range(2):
                        bank = PA0 if h2 == 0 else PA1
                        for qtl in range(4):
                            dcol = bank[:, qtl * 65 + 64: qtl * 65 + 65]
                            ix = h2 * 4 + qtl
                            kb.op("v", lambda e, d=dcol, ix=ix: e.reciprocal(
                                s_rden[:, ix:ix + 1], d))
                    kb.dinc("v", SV)
                    for h2 in range(2):
                        bank = PA0 if h2 == 0 else PA1
                        for qtl in range(4):
                            ix = h2 * 4 + qtl
                            src2 = bank[:, qtl * 65: qtl * 65 + 64]
                            dst2 = avn[h2][:, qtl * 64:(qtl + 1) * 64]
                            kb.op("v", lambda e, d=dst2, s_=src2, ix=ix:
                                  e.tensor_scalar(d, s_,
                                                  s_rden[:, ix:ix + 1],
                                                  None, ALU.mult))
                    kb.dinc("v", SV)
                    av_free_sv = SV.n
                    PT = PT0 if hp % 2 == 0 else PT1
                    kb.wait("t", SV)
                    kb.tmode("trans")
                    for h2 in range(2):
                        for qtl in range(4):
                            src2 = avn[h2][:, qtl * 64:(qtl + 1) * 64]
                            dst2 = PT[h2 * 64:h2 * 64 + 64,
                                      qtl * 128:(qtl + 1) * 128]
                            lastt = (h2 == 1 and qtl == 3)
                            kb.op("t", lambda e, d=dst2, s_=src2, h2=h2:
                                  e.transpose(d, s_, idb[:],
                                              tile_position=(0, h2 * 64)),
                                  inc=(ST if lastt else None))
                    kb.wait("v", ST)
                    dst3 = avfm[:, hp * 512:(hp + 1) * 512]
                    kb.op("v", lambda e, d=dst3, s_=PT: e.tensor_copy(
                        d, s_[:, 0:512]))
                    kb.dinc("v", SV)
                kb.wait("t", SV)
                kb.tmode("full")
                for qtl in range(4):
                    qt_g = qb * 4 + qtl
                    for ob in range(2):
                        bi, bank = next_bank()
                        for ch in range(NCH):
                            lhsT = avfm[:, ch * 512 + qtl * 128:
                                        ch * 512 + (qtl + 1) * 128]
                            rhs = wo_res[:, ch * 1024 + ob * 512:
                                         ch * 1024 + ob * 512 + 512]
                            last_mm = (ch == NCH - 1) and not with_bias_row
                            kb.op("t", lambda e, b=bank, l=lhsT, r=rhs,
                                  st=(ch == 0), sp=last_mm: e.matmul(
                                      b, l, r, start=st, stop=sp,
                                      skip_group_check=True),
                                  inc=(ST if last_mm else None))
                        if with_bias_row:
                            rhs = bias_row[:, ob * 512:(ob + 1) * 512]
                            kb.op("t", lambda e, b=bank, l=c128[:], r=rhs:
                                  e.matmul(b, l, r, start=False, stop=True,
                                           skip_group_check=True), inc=ST)
                        kb.wait("v", ST)
                        dstr = res[:, qt_g * C + ob * 512:
                                   qt_g * C + ob * 512 + 512]
                        kb.op("v", lambda e, d=dstr, p=bank: e.tensor_tensor(
                            d, d, p, ALU.add), inc=SV)
                        free_bank(bi, SV)
                ps_sv[0] = SV.n
                ps_sv[1] = SV.n
            kb.dinc("v", SV)

        def attn_a():
            ln_and_transpose(first=(sub_idx[0] == 0))
            if debug:
                kb.wait("y", SV)
                dma_y(dbg_st[:], stats[:])
                dma_y(dbg_xtm[:], B1[:])

            def xnT_rhs(ci, qb):
                return B2[:, ci * T_LEN + qb * 512: ci * T_LEN + qb * 512 + 512]

            def xnT_lhs(ci, qt):
                return B2[:, ci * T_LEN + qt * 128: ci * T_LEN + qt * 128 + 128]

            kb.wait("v", ST)
            kb.op("v", lambda e: e.memset(vt_ones, 1.0))
            kb.dinc("v", SV)
            proj_wstat(wkT, NCH, NCH, xnT_rhs,
                       evac_to_fm(KF, 0, "s", SS), C)
            kb.dinc("s", SS)
            dma_g(B1.reshape([128, 16, 1024])[:, 0:8, :], whole_hbm(wvT),
                  guard=ST)
            wv_dw = DW.n
            kb.wait("t", DW, wv_dw)
            proj_v(xnT_lhs, NCH, B1[:, 0:8192])
            proj_wstat(wqT, NCH, NCH, xnT_rhs,
                       evac_to_fm(B1, 0, "v", SV), C)
            kb.dinc("v", SV)
            dst = bass.AP(B2, 6656, [[NT * C, 128], [1024, 8], [1, 1024]])
            dma_g(dst, whole_hbm(woT), guard=ST)
            kb.wait("t", DW)
            kb.wait("t", SV)
            kb.wait("t", SS)
            if debug:
                kb.wait("y", SV)
                kb.wait("y", SS)
                dma_y(dbg_xn[:], B2[:])
                dma_y(dbg_q[:], B1[:])
                dma_y(dbg_k[:], KF[:])
                dma_y(dbg_vt[:], VT[:])
            attention(B1, KF, with_bias_row=False)

        def ffn():
            ln_and_transpose(first=False)

            def xnT_lhs(ci, qt):
                return B2[:, ci * T_LEN + qt * 128: ci * T_LEN + qt * 128 + 128]

            if not ffn_state["q"]:
                dma_y(bass.AP(KFf, 0, [[8192, 128], [1024, 8], [1, 1024]]),
                      whole_hbm(fwT), guard=ST)
                kb.wait("v", DY)
                kb.wait("v", ST)
                w_all = KFf[:, 0:8192]
                kb.op("v", lambda e: e.tensor_reduce(
                    s_tot, w_all, AXL.X, ALU.add, apply_absolute_value=True))
                kb.dinc("v", SV)
                kb.wait("t", SV)
                kb.tmode("fp32a")
                kb.op("t", lambda e: e.matmul(
                    PA0[0:1, 0:1], onesf[:, 0:1], s_tot, start=True,
                    stop=True, skip_group_check=True), inc=ST)
                kb.wait("v", ST)
                kb.op("v", lambda e: e.tensor_copy(s_msq[0:1, 0:1],
                                                   PA0[0:1, 0:1]))
                kb.dinc("v", SV)
                kb.wait("t", SV)
                kb.tmode("fp32b")
                kb.op("t", lambda e: e.matmul(
                    PA1[:, 0:1], onesf[0:1, :], s_msq[0:1, 0:1], start=True,
                    stop=True, skip_group_check=True), inc=ST)
                kb.wait("v", ST)
                kb.op("v", lambda e: e.tensor_scalar(
                    s_scale, PA1[:, 0:1], 1.0 / (C * C), 1e-5,
                    ALU.mult, ALU.max))
                kb.dinc("v", SV)
                kb.op("v", lambda e: e.reciprocal(s_rscale, s_scale))
                kb.dinc("v", SV)
                u = VTf[:, 0:8192]
                kb.op("v", lambda e: e.tensor_scalar(
                    u, w_all, s_rscale[:, 0:1], None, ALU.mult))
                kb.dinc("v", SV)
                g_ = B1f[:, 0:8192]
                l_ = KFf[:, 0:8192]
                kb.op("v", lambda e: e.tensor_scalar(g_, u, 0.5, None,
                                                     ALU.is_gt))
                kb.op("v", lambda e: e.tensor_scalar(l_, u, -0.5, None,
                                                     ALU.is_lt))
                kb.dinc("v", SV)
                kb.op("v", lambda e: e.tensor_tensor(
                    VT8[:, 0:8192], g_, l_, ALU.subtract))
                kb.dinc("v", SV)
                kb.wait("y", SV)
                dma_y(whole_hbm(wt8),
                      bass.AP(VT8, 0, [[2 * VT_COLS, 128], [1024, 8],
                                       [1, 1024]]))
                ffn_state["q"] = True
                ffn_state["store_dy"] = DY.n
            else:
                dma_y(bass.AP(VT8, 0, [[2 * VT_COLS, 128], [1024, 8],
                                       [1, 1024]]),
                      whole_hbm(wt8), guard=ST)
            wt_dy = DY.n
            kb.wait("t", DY, wt_dy)
            for qt in range(NT):
                for ob in range(2):
                    bi, bank = next_bank()
                    for ci in range(NCH):
                        lhsT = xnT_lhs(ci, qt)
                        rhs = VT8[:, ci * 1024 + ob * 512:
                                  ci * 1024 + ob * 512 + 512]
                        kb.op("t", lambda e, b=bank, l=lhsT, r=rhs,
                              st=(ci == 0), sp=(ci == NCH - 1): e.matmul(
                                  b, l, r, start=st, stop=sp,
                                  skip_group_check=True),
                              inc=(ST if ci == NCH - 1 else None))
                    kb.wait("v", ST)
                    dstr = res[:, qt * C + ob * 512: qt * C + ob * 512 + 512]
                    kb.op("v", lambda e, d=dstr, p=bank:
                          e.scalar_tensor_tensor(d, p, s_scale[:, 0:1], d,
                                                 ALU.mult, ALU.add), inc=SV)
                    free_bank(bi, SV)
            kb.dinc("v", SV)
            if ffn_state.get("store_dy"):
                kb.wait("v", DY, ffn_state["store_dy"])
                ffn_state["store_dy"] = 0

        def attn_b():
            ln_and_transpose(first=False)

            def xnT_rhs(ci, qb):
                return B2[:, ci * T_LEN + qb * 512: ci * T_LEN + qb * 512 + 512]

            kb.wait("v", ST)
            kb.op("v", lambda e: e.memset(vt_ones, 1.0))
            kb.dinc("v", SV)
            proj_wstat(akd, NCH, 4, xnT_rhs,
                       evac_to_fm(B1, 0, "s", SS, bias_col=bkd_t,
                                  func=AF.Gelu), C)
            kb.dinc("s", SS)
            kb.wait("v", SS)
            for lc in range(2):
                dst = B1[:, (4 + lc) * T_LEN:(5 + lc) * T_LEN]
                a_ = B1[:, lc * T_LEN:(lc + 1) * T_LEN]
                b_ = B1[:, (2 + lc) * T_LEN:(3 + lc) * T_LEN]
                kb.op("v", lambda e, d=dst, a=a_, b=b_: e.tensor_tensor(
                    d, a, b, ALU.add))
            kb.dinc("v", SV)

            def lat_rhs(ci, qb):
                base = (4 + ci) * T_LEN + qb * 512
                return B1[:, base:base + 512]

            def lat_lhs(ci, qt):
                base = (4 + ci) * T_LEN + qt * 128
                return B1[:, base:base + 128]

            kb.wait("t", SV)
            proj_wstat(aku, 2, NCH, lat_rhs,
                       evac_to_fm(KF, 0, "s", SS, bias_col=bkuk_t,
                                  func=AF.Identity), LAT, base_col=0)
            kb.dinc("s", SS)
            dma_g(B1.reshape([128, 16, 1024])[:, 0:2, :],
                  aku[:, C:2 * C].rearrange("(ci p) co -> p ci co", p=128),
                  guard=SV)
            kb.wait("t", DW)
            kb.wait("t", SS)
            proj_v(lat_lhs, 2, B1[:, 0:2048])
            proj_wstat(aq, NCH, NCH, xnT_rhs,
                       evac_to_fm(B1, 0, "s", SS, bias_col=bq_t,
                                  func=AF.Identity), C)
            kb.dinc("s", SS)
            dst = bass.AP(B2, 6656, [[NT * C, 128], [1024, 8], [1, 1024]])
            dma_g(dst, whole_hbm(awo), guard=ST)
            kb.wait("t", DW)
            kb.tmode("full")
            for ob in range(2):
                bank = PA0[:, 0:512] if ob == 0 else PA1[:, 0:512]
                for ci in range(NCH):
                    lhsT = foldt[:, ci * 128:(ci + 1) * 128]
                    rhs = wo_res[:, ci * 1024 + ob * 512:
                                 ci * 1024 + ob * 512 + 512]
                    kb.op("t", lambda e, b=bank, l=lhsT, r=rhs, st=(ci == 0),
                          sp=(ci == NCH - 1): e.matmul(
                              b, l, r, start=st, stop=sp,
                              skip_group_check=True),
                          inc=(ST if ci == NCH - 1 else None))
                kb.wait("v", ST)
                dst2 = bias_row[:, ob * 512:(ob + 1) * 512]
                kb.op("v", lambda e, d=dst2, p=bank, ob=ob: e.tensor_tensor(
                    d, p, bo_bf[:, ob * 512:(ob + 1) * 512], ALU.add))
            kb.dinc("v", SV)
            kb.wait("t", SV)
            kb.wait("t", SS)
            attention(B1, KF, with_bias_row=True)

        ffn_state = {"q": False, "store_dy": 0}
        sub_idx = [0]
        phases = [attn_a, ffn, attn_b, ffn]
        for seq in range(2):
            if sub_idx[0] >= n_sublayers:
                break
            if seq == 1:
                kb.wait("y", SV)
                DO.n += 16
                kb.emit("y", lambda e: e.dma_start(
                    y0.rearrange("(t p) c -> p t c", p=128),
                    res3).then_inc(DO.h, 16))
                kb.wait("y", DO)
                dma_y(res3,
                      x1.rearrange("(t p) c -> p t c", p=128))
                xdy = DY.n
                kb.wait("v", DY, xdy)
                kb.wait("s", DY, xdy)
            for ph in phases:
                if sub_idx[0] >= n_sublayers:
                    break
                ph()
                sub_idx[0] += 1
        ran_seq1 = sub_idx[0] > 4
        kb.wait("y", SV)
        if ran_seq1:
            DO.n += 16
            kb.emit("y", lambda e: e.dma_start(
                y1.rearrange("(t p) c -> p t c", p=128),
                res3).then_inc(DO.h, 16))
        else:
            DO.n += 16
            kb.emit("y", lambda e: e.dma_start(
                y0.rearrange("(t p) c -> p t c", p=128),
                res3).then_inc(DO.h, 16))
            DO.n += 16
            kb.emit("y", lambda e: e.dma_start(
                y1.rearrange("(t p) c -> p t c", p=128),
                res3).then_inc(DO.h, 16))
        kb.wait("y", DO)

        with nc.Block() as block:
            @block.tensor
            def _(t):
                for f in kb.ops["t"]:
                    f(t)

            @block.vector
            def _(v):
                for f in kb.ops["v"]:
                    f(v)

            @block.scalar
            def _(s):
                for f in kb.ops["s"]:
                    f(s)

            @block.gpsimd
            def _(g):
                for f in kb.ops["g"]:
                    f(g)

            @block.sync
            def _(y):
                for f in kb.ops["y"]:
                    f(y)

    return nc


def _shared_inputs(inputs):
    d = inputs

    def a(x):
        return np.ascontiguousarray(np.asarray(x, dtype=np.float32))

    return {
        "wqT": a(np.asarray(d["L0_attn_a_wq"]).T),
        "wkT": a(np.asarray(d["L0_attn_a_wk"]).T),
        "wvT": a(np.asarray(d["L0_attn_a_wv"]).T),
        "woT": a(np.asarray(d["L0_attn_a_wo"]).T),
        "aq": a(d["L0_attn_b_Wq"]),
        "akd": a(d["L0_attn_b_Wkd"]),
        "aku": a(d["L0_attn_b_Wku"]),
        "awo": a(d["L0_attn_b_Wo"]),
        "fwT": a(np.asarray(d["L0_ffn_weight"]).T),
        "bq": a(np.asarray(d["L0_attn_b_bq"]).reshape(8, 128).T),
        "bkd": a(np.asarray(d["L0_attn_b_bkd"]).reshape(4, 128).T),
        "bkuk": a(np.asarray(d["L0_attn_b_bku"])[:1024].reshape(8, 128).T),
        "bkuv": a(np.broadcast_to(
            np.asarray(d["L0_attn_b_bku"])[1024:].reshape(8, 128).T[:, :, None],
            (128, 8, 128)).reshape(128, 1024)),
        "bo": a(np.broadcast_to(
            np.asarray(d["L0_attn_b_bo"])[None, :], (128, 1024))),
    }


def kernel(**inputs):
    nc = build_program()
    sh = _shared_inputs(inputs)
    x = np.asarray(inputs["L0_attn_a_x"], dtype=np.float32)
    in_maps = []
    for core in range(8):
        m = dict(sh)
        m["x0"] = np.ascontiguousarray(x[2 * core])
        m["x1"] = np.ascontiguousarray(x[2 * core + 1])
        in_maps.append(m)
    res = run_bass_kernel_spmd(nc, in_maps, core_ids=list(range(8)))
    out = np.empty((16, T_LEN, C), dtype=np.float32)
    for core in range(8):
        out[2 * core] = res.results[core]["y0"]
        out[2 * core + 1] = res.results[core]["y1"]
    return out


# version 05
# speedup vs baseline: 1.5972x; 1.0071x over previous
import numpy as np
import concourse.bass as bass
import concourse.mybir as mybir
import bass_rust as _br
from concourse.bass_utils import run_bass_kernel_spmd
from contextlib import ExitStack

F32 = mybir.dt.float32
BF16 = mybir.dt.bfloat16
FP8 = mybir.dt.float8e4
I32 = mybir.dt.int32
AF = mybir.ActivationFunctionType
ALU = mybir.AluOpType
AXL = mybir.AxisListType

T_LEN = 2048
C = 1024
NT = 16
NCH = 8
NQB = 4
NHP = 8
LAT = 256
EPS = 1e-5
ATT_SCALE = 0.125
VT_COLS = 16 * 16 * 65


class Ctr:
    def __init__(self, h):
        self.h = h
        self.n = 0


class KB:
    ENGS = ("t", "v", "s", "g", "y")

    def __init__(self):
        self.ops = {e: [] for e in self.ENGS}
        self.tmode_cur = [None]

    def emit(self, eng, fn):
        self.ops[eng].append(fn)

    def wait(self, eng, ctr, val=None):
        v = ctr.n if val is None else val
        if v <= 0:
            return
        h = ctr.h
        self.emit(eng, lambda e: e.wait_ge(h, v))

    def op(self, eng, fn, inc=None, drain=True):
        if inc is None:
            self.emit(eng, fn)
        elif eng == "t" or not drain:
            inc.n += 1
            h = inc.h
            self.emit(eng, lambda e: fn(e).then_inc(h, 1))
        else:
            inc.n += 1
            h = inc.h
            self.emit(eng, fn)
            self.emit(eng, lambda e: e.drain(fusable=False).then_inc(h, 1))

    def dinc(self, eng, ctr):
        ctr.n += 1
        h = ctr.h
        self.emit(eng, lambda e: e.drain(fusable=False).then_inc(h, 1))

    TMODE_DRAINS = False
    T_DRAIN_INCS = False

    def tmode(self, m):
        if self.tmode_cur[0] != m:
            if self.TMODE_DRAINS and self.tmode_cur[0] is not None:
                self.emit("t", lambda e: e.drain(fusable=False))
            self.tmode_cur[0] = m


def build_program(n_sublayers=8, debug=False):
    nc = bass.Bass("TRN2", target_bir_lowering=False)

    x0 = nc.dram_tensor("x0", [T_LEN, C], F32, kind="ExternalInput")
    x1 = nc.dram_tensor("x1", [T_LEN, C], F32, kind="ExternalInput")
    wqT = nc.dram_tensor("wqT", [C, C], F32, kind="ExternalInput")
    wkT = nc.dram_tensor("wkT", [C, C], F32, kind="ExternalInput")
    wvT = nc.dram_tensor("wvT", [C, C], F32, kind="ExternalInput")
    woT = nc.dram_tensor("woT", [C, C], F32, kind="ExternalInput")
    aq = nc.dram_tensor("aq", [C, C], F32, kind="ExternalInput")
    akd = nc.dram_tensor("akd", [C, 512], F32, kind="ExternalInput")
    aku = nc.dram_tensor("aku", [LAT, 2 * C], F32, kind="ExternalInput")
    awo = nc.dram_tensor("awo", [C, C], F32, kind="ExternalInput")
    fwT = nc.dram_tensor("fwT", [C, C], F32, kind="ExternalInput")
    bq_h = nc.dram_tensor("bq", [128, 8], F32, kind="ExternalInput")
    bkd_h = nc.dram_tensor("bkd", [128, 4], F32, kind="ExternalInput")
    bkuk_h = nc.dram_tensor("bkuk", [128, 8], F32, kind="ExternalInput")
    bkuv_h = nc.dram_tensor("bkuv", [128, 1024], F32, kind="ExternalInput")
    bo_h = nc.dram_tensor("bo", [128, C], F32, kind="ExternalInput")
    if debug:
        dbg_st = nc.dram_tensor("dbg_st", [128, 128], F32,
                                kind="ExternalOutput")
        dbg_xtm = nc.dram_tensor("dbg_xtm", [128, NT * C], BF16,
                                 kind="ExternalOutput")
        dbg_xn = nc.dram_tensor("dbg_xn", [128, NT * C], BF16,
                                kind="ExternalOutput")
        dbg_q = nc.dram_tensor("dbg_q", [128, NT * C], BF16,
                               kind="ExternalOutput")
        dbg_k = nc.dram_tensor("dbg_k", [128, NT * C], BF16,
                               kind="ExternalOutput")
        dbg_vt = nc.dram_tensor("dbg_vt", [128, VT_COLS], BF16,
                                kind="ExternalOutput")
    y0 = nc.dram_tensor("y0", [T_LEN, C], F32, kind="ExternalOutput")
    y1 = nc.dram_tensor("y1", [T_LEN, C], F32, kind="ExternalOutput")
    wt8 = nc.dram_tensor("wt8", [C, C], FP8)

    with ExitStack() as ctx:
        en = ctx.enter_context
        DW = Ctr(en(nc.semaphore("DW")))
        DY = Ctr(en(nc.semaphore("DY")))
        DO = Ctr(en(nc.semaphore("DO")))
        ST = Ctr(en(nc.semaphore("ST")))
        SV = Ctr(en(nc.semaphore("SV")))
        SS = Ctr(en(nc.semaphore("SS")))
        SG = Ctr(en(nc.semaphore("SG")))

        res = en(nc.sbuf_tensor("res", [128, NT * C], F32))
        B1 = en(nc.sbuf_tensor("B1", [128, NT * C], BF16))
        B2 = en(nc.sbuf_tensor("B2", [128, NT * C], BF16))
        KF = en(nc.sbuf_tensor("KF", [128, NT * C], BF16))
        VT = en(nc.sbuf_tensor("VT", [128, VT_COLS], BF16))
        WS = en(nc.sbuf_tensor("WS", [128, 2 * 1024], BF16))
        idb = en(nc.sbuf_tensor("idb", [128, 128], BF16))
        maskb = en(nc.sbuf_tensor("maskb", [128, 128], BF16))
        onesr = en(nc.sbuf_tensor("onesr", [128, 128], BF16))
        stats = en(nc.sbuf_tensor("stats", [128, 128], F32))
        bo_bf = en(nc.sbuf_tensor("bo_bf", [128, 1024], BF16))
        foldt = en(nc.sbuf_tensor("foldt", [128, 1024], BF16))
        c128 = en(nc.sbuf_tensor("c128", [128, 128], BF16))
        bq_t = en(nc.sbuf_tensor("bq_t", [128, 8], F32))
        bkd_t = en(nc.sbuf_tensor("bkd_t", [128, 4], F32))
        bkuk_t = en(nc.sbuf_tensor("bkuk_t", [128, 8], F32))
        onesf = en(nc.sbuf_tensor("onesf", [128, 128], F32))

        PS0 = en(nc.psum_tensor("PS0", [128, 1024], F32))
        PS1 = en(nc.psum_tensor("PS1", [128, 1024], F32))
        PA0 = en(nc.psum_tensor("PA0", [128, 512], F32))
        PA1 = en(nc.psum_tensor("PA1", [128, 512], F32))
        PT0 = en(nc.psum_tensor("PT0", [128, 1024], BF16))
        PT1 = en(nc.psum_tensor("PT1", [128, 1024], BF16))

        s_sumx = stats[:, 0:16]
        s_sumxx = stats[:, 16:32]
        s_m = stats[:, 32:48]
        s_var = stats[:, 48:64]
        s_r = stats[:, 64:80]
        s_nmr = stats[:, 80:96]
        s_eps = stats[:, 96:97]
        s_scale = stats[:, 97:98]
        s_rscale = stats[:, 98:99]
        s_tot = stats[:, 99:100]
        s_rden = stats[:, 100:108]
        s_msq = stats[:, 108:124]
        s_zero = stats[:, 124:125]

        B1f = B1.bitcast(F32)
        KFf = KF.bitcast(F32)
        VTf = VT.bitcast(F32)
        VT8 = VT.bitcast(FP8)
        B1i = B1.bitcast(I32)

        pb = [B2[:, 0:1024], B2[:, 1024:2048]]
        avfm = B2[:, 2048:6144]
        avn = [B2[:, 6144:6400], B2[:, 6400:6656]]
        wo_res = B2[:, 6656:14848]
        bias_row = B2[:, 14848:15872]
        scr1 = B2[:, 0:1024]
        scr2 = B2[:, 1024:2048]

        kb = KB()
        res3 = bass.AP(res, 0, [[NT * C, 128], [C, NT], [1, C]])

        def dma_g(dst_ap, src_ap, guard=None):
            if guard is not None:
                kb.wait("g", guard)
            DW.n += 16
            kb.emit("g", lambda e: e.dma_start(dst_ap, src_ap).then_inc(DW.h, 16))

        def dma_y(dst_ap, src_ap, guard=None):
            if guard is not None:
                kb.wait("y", guard)
            DY.n += 16
            kb.emit("y", lambda e: e.dma_start(dst_ap, src_ap).then_inc(DY.h, 16))

        def col_slices_hbm(w2d, co, rows, base_col=0):
            a = base_col + co * 128
            return w2d[:, a:a + 128].rearrange("(ci p) co -> p ci co", p=128)

        def whole_hbm(w2d):
            return w2d.rearrange("(ci p) co -> p ci co", p=128)

        gen_banks = [PS0[:, 0:512], PS0[:, 512:1024],
                     PS1[:, 0:512], PS1[:, 512:1024]]
        gb = {"i": 0, "free": [(0, None)] * 4}

        def next_bank():
            i = gb["i"] % 4
            gb["i"] += 1
            v, ctr = gb["free"][i]
            if ctr is not None:
                kb.wait("t", ctr, v)
            return i, gen_banks[i]

        def free_bank(i, ctr):
            gb["free"][i] = (ctr.n, ctr)

        kb.op("g", lambda e: e.iota(B1i[:, 0:128], [[-1, 128]], base=0,
                                    channel_multiplier=1), inc=SG)
        kb.wait("v", SG)
        kb.op("v", lambda e: e.tensor_scalar(idb[:], B1i[:, 0:128], 0.0, None,
                                             ALU.is_equal))
        kb.op("v", lambda e: e.tensor_scalar(maskb[:], B1i[:, 0:128], 0.0, None,
                                             ALU.is_le))
        kb.op("v", lambda e: e.memset(onesr[0:1, :], 1.0))
        kb.op("v", lambda e: e.memset(s_eps, EPS))
        kb.op("v", lambda e: e.memset(s_zero, 0.0))
        kb.op("v", lambda e: e.memset(onesf[:], 1.0))
        kb.op("v", lambda e: e.memset(c128[:], 1.0 / 128.0))
        vt_ones = bass.AP(VT, 64, [[VT_COLS, 128], [65, 256]])
        kb.op("v", lambda e: e.memset(vt_ones, 1.0))
        kb.dinc("v", SV)
        dma_y(bq_t[:], bq_h[:])
        dma_y(bkd_t[:], bkd_h[:])
        dma_y(bkuk_t[:], bkuk_h[:])
        dma_g(foldt[:], bkuv_h[:])
        dma_g(bo_bf[:], bo_h[:])
        dma_y(res3, x0.rearrange("(t p) c -> p t c", p=128))
        setup_dy = DY.n
        setup_dw = DW.n

        def ln_and_transpose(first):
            kb.wait("s", ST)
            kb.wait("v", ST)
            if first:
                kb.wait("v", DY, setup_dy)
                kb.wait("s", DY, setup_dy)
                kb.wait("t", DW, setup_dw)
            kb.wait("s", SV)
            for t in range(NT):
                rt = res[:, t * C:(t + 1) * C]
                kb.op("v", lambda e, rt=rt, t=t: e.tensor_scalar(
                    scr1[:], rt, 1.0, None, ALU.mult, op1=ALU.add,
                    accum_out=s_sumx[:, t:t + 1]))
                kb.op("s", lambda e, rt=rt, t=t: e.activation(
                    scr2[:], rt, AF.Square, accum_out=s_sumxx[:, t:t + 1]))
            kb.dinc("v", SV)
            kb.dinc("s", SS)
            kb.wait("v", SS)
            kb.op("v", lambda e: e.tensor_scalar(s_m, s_sumx, 1.0 / C, None,
                                                 ALU.mult))
            kb.dinc("v", SV)
            kb.op("v", lambda e: e.tensor_tensor(s_msq, s_m, s_m, ALU.mult))
            kb.dinc("v", SV)
            kb.op("v", lambda e: e.scalar_tensor_tensor(
                s_var, s_sumxx, 1.0 / C, s_msq, ALU.mult, ALU.subtract))
            kb.dinc("v", SV)
            kb.wait("s", SV)
            kb.op("s", lambda e: e.activation(s_r, s_var, AF.Ln,
                                              bias=s_eps, scale=1.0))
            kb.dinc("s", SS)
            kb.op("s", lambda e: e.activation(s_r, s_r, AF.Exp, scale=-0.5))
            kb.dinc("s", SS)
            kb.wait("v", SS)
            kb.op("v", lambda e: e.scalar_tensor_tensor(
                s_nmr, s_m, -1.0, s_r, ALU.mult, ALU.mult))
            kb.dinc("v", SV)
            for t in range(NT):
                rt = res[:, t * C:(t + 1) * C]
                xt = B1[:, t * C:(t + 1) * C]
                kb.op("v", lambda e, rt=rt, xt=xt, t=t: e.tensor_scalar(
                    xt, rt, s_r[:, t:t + 1], s_nmr[:, t:t + 1],
                    ALU.mult, ALU.add))
            kb.dinc("v", SV)
            kb.wait("t", SV)
            kb.tmode("trans")
            for t in range(NT):
                PT = PT0 if t % 2 == 0 else PT1
                if t >= 2:
                    kb.wait("t", SV)
                for ch in range(NCH):
                    src = B1[:, t * C + ch * 128: t * C + (ch + 1) * 128]
                    dst = PT[:, ch * 128:(ch + 1) * 128]
                    kb.op("t", lambda e, d=dst, s_=src: e.transpose(
                        d, s_, idb[:]), inc=(ST if ch == NCH - 1 else None))
                kb.wait("v", ST)
                src = bass.AP(PT, 0, [[1024, 128], [128, NCH], [1, 128]])
                dst = bass.AP(B2, t * 128, [[NT * C, 128], [T_LEN, NCH],
                                            [1, 128]])
                kb.op("v", lambda e, d=dst, s_=src: e.tensor_copy(d, s_),
                      inc=SV, drain=False)
            kb.dinc("v", SV)
            kb.wait("t", SV)

        def proj_wstat(w_hbm, n_ci, n_co, rhs_col0, evac_fn, w_rows,
                       base_col=0):
            slot = [WS[:, 0:1024], WS[:, 1024:2048]]
            slot_rows = w_rows // 128
            mm_done = {}
            dw_at = {}
            def slot3(i):
                return bass.AP(WS, i * 1024,
                               [[2048, 128], [128, slot_rows], [1, 128]])

            for co in range(min(2, n_co)):
                dma_g(slot3(co % 2), col_slices_hbm(w_hbm, co, w_rows,
                                                    base_col), guard=ST)
                dw_at[co] = DW.n
            kb.tmode("full")
            for co in range(n_co):
                kb.wait("t", DW, dw_at[co])
                sl = slot[co % 2]
                for qb in range(NQB):
                    bi, bank = next_bank()
                    for ci in range(n_ci):
                        lhsT = sl[:, ci * 128:(ci + 1) * 128]
                        rhs = rhs_col0(ci, qb)
                        kb.op("t", lambda e, b=bank, l=lhsT, r=rhs,
                              st=(ci == 0), sp=(ci == n_ci - 1): e.matmul(
                                  b, l, r, start=st, stop=sp,
                                  skip_group_check=True),
                              inc=(ST if ci == n_ci - 1 else None))
                    evac_fn(co, qb, bank, bi)
                mm_done[co] = ST.n
                nxt = co + 2
                if nxt < n_co:
                    kb.wait("g", ST, mm_done[co])
                    dma_g(slot3(nxt % 2),
                          col_slices_hbm(w_hbm, nxt, w_rows, base_col))
                    dw_at[nxt] = DW.n

        def evac_to_fm(dst_tensor, dst_base, eng, sem, bias_col=None,
                       func=AF.Identity):
            def f(co, qb, bank, bi):
                dst = dst_tensor[:, dst_base + co * T_LEN + qb * 512:
                                 dst_base + co * T_LEN + qb * 512 + 512]
                kb.wait(eng, ST)
                if func is not AF.Identity:
                    b = (bias_col[:, co:co + 1] if bias_col is not None
                         else s_zero)
                    kb.op(eng, lambda e, d=dst, p=bank, b=b: e.activation(
                        d, p, func, bias=b, scale=1.0), inc=sem, drain=False)
                elif bias_col is not None:
                    b = bias_col[:, co:co + 1]
                    kb.op("v", lambda e, d=dst, p=bank, b=b: e.tensor_scalar(
                        d, p, b, None, ALU.add), inc=sem, drain=False)
                else:
                    kb.op("v", lambda e, d=dst, p=bank: e.tensor_copy(d, p),
                          inc=sem, drain=False)
                free_bank(bi, sem)
            return f

        def proj_v(lhs_col0, n_ci, rhs_res):
            kb.tmode("full")
            for qt in range(NT):
                for eh in range(2):
                    bi, bank = next_bank()
                    for ci in range(n_ci):
                        lhsT = lhs_col0(ci, qt)
                        rhs = rhs_res[:, ci * 1024 + eh * 512:
                                      ci * 1024 + eh * 512 + 512]
                        kb.op("t", lambda e, b=bank, l=lhsT, r=rhs,
                              st=(ci == 0), sp=(ci == n_ci - 1): e.matmul(
                                  b, l, r, start=st, stop=sp,
                                  skip_group_check=True),
                              inc=(ST if ci == n_ci - 1 else None))
                    kb.wait("v", ST)
                    dst = bass.AP(VT, (qt * 16 + eh * 8) * 65,
                                  [[VT_COLS, 128], [65, 8], [1, 64]])
                    src = bass.AP(bank.tensor, bank.offset,
                                  [[bank.tensor.shape[1], 128], [64, 8],
                                   [1, 64]])
                    kb.op("v", lambda e, d=dst, s_=src: e.tensor_copy(d, s_),
                          inc=SV, drain=False)
                    free_bank(bi, SV)

        def attention(q_t, k_t, with_bias_row):
            kc_par = {"i": 0}
            ps_exp = [0, 0]
            ps_sv = [0, 0]
            av_free_sv = SV.n

            for qb in range(NQB):
                n_kc = 4 * (qb + 1)
                items = [(hp, kc) for hp in range(NHP) for kc in range(n_kc)]
                sc = {}

                def emit_scores(i, qb=qb, n_kc=n_kc, items=items, sc=sc):
                    hp, kc = items[i]
                    buf = kc_par["i"] % 2
                    kc_par["i"] += 1
                    PS = PS0 if buf == 0 else PS1
                    diag = kc >= 4 * qb
                    off = (kc - 4 * qb) * 128 if diag else 0
                    w = 512 - off
                    kb.tmode("row64")
                    kb.wait("t", SS, ps_exp[buf])
                    if ps_sv[buf]:
                        kb.wait("t", SV, ps_sv[buf])
                        ps_sv[buf] = 0
                    for h2 in range(2):
                        rb = h2 * 64
                        kcol = hp * T_LEN + kc * 128
                        qcol = hp * T_LEN + qb * 512 + off
                        lhsT = k_t[rb:rb + 64, kcol:kcol + 128]
                        rhs = q_t[rb:rb + 64, qcol:qcol + w]
                        out = PS[:, h2 * 512 + off: h2 * 512 + off + w]
                        kb.op("t", lambda e, o=out, l=lhsT, r=rhs, rb=rb:
                              e.matmul(o, l, r, start=True, stop=True,
                                       skip_group_check=True,
                                       tile_position=(rb, 0)),
                              inc=(ST if h2 == 1 else None))
                    sc[i] = (buf, off, w, ST.n, PS)

                emit_scores(0)
                for i, (hp, kc) in enumerate(items):
                    if i + 1 < len(items):
                        emit_scores(i + 1)
                    buf, off, w, st_sc, PS = sc.pop(i)
                    diag = kc >= 4 * qb
                    kb.wait("s", ST, st_sc)
                    src = bass.AP(PS, off, [[1024, 128], [512, 2], [1, w]])
                    dst = bass.AP(pb[buf].tensor, pb[buf].offset + off,
                                  [[NT * C, 128], [512, 2], [1, w]])
                    kb.op("s", lambda e, d=dst, s_=src: e.activation(
                        d, s_, AF.Exp, scale=ATT_SCALE))
                    kb.dinc("s", SS)
                    ps_exp[buf] = SS.n
                    if diag:
                        kb.wait("v", SS)
                        mt = bass.AP(pb[buf].tensor, pb[buf].offset + off,
                                     [[NT * C, 128], [512, 2], [1, 128]])
                        mb = bass.AP(maskb, 0,
                                     [[128, 128], [0, 2], [1, 128]])
                        kb.op("v", lambda e, m=mt, mb=mb: e.tensor_tensor(
                            m, m, mb, ALU.mult))
                        kb.dinc("v", SV)
                        mask_sv = SV.n
                    if diag:
                        kb.wait("t", SV, mask_sv)
                    else:
                        kb.wait("t", SS)
                    if kc == 0 and av_free_sv:
                        kb.wait("t", SV, av_free_sv)
                    kb.tmode("full")
                    qt_lo = max(0, kc - 4 * qb)
                    for h2 in range(2):
                        h = 2 * hp + h2
                        bank = PA0 if h2 == 0 else PA1
                        for qtl in range(qt_lo, 4):
                            lhsT = pb[buf][:, h2 * 512 + qtl * 128:
                                           h2 * 512 + (qtl + 1) * 128]
                            rhs = bass.AP(VT, (kc * 16 + h) * 65,
                                          [[VT_COLS, 128], [1, 65]])
                            out = bank[:, qtl * 65: qtl * 65 + 65]
                            first = (kc == 0 and qtl == 0)
                            last = (kc == n_kc - 1 and qtl == 3 and h2 == 1)
                            kb.op("t", lambda e, o=out, l=lhsT, r=rhs,
                                  st=first: e.matmul(
                                      o, l, r, start=st, stop=False,
                                      skip_group_check=True),
                                  inc=(ST if last else None))
                    if kc != n_kc - 1:
                        continue
                    kb.wait("v", ST)
                    for h2 in range(2):
                        bank = PA0 if h2 == 0 else PA1
                        for qtl in range(4):
                            dcol = bank[:, qtl * 65 + 64: qtl * 65 + 65]
                            ix = h2 * 4 + qtl
                            kb.op("v", lambda e, d=dcol, ix=ix: e.reciprocal(
                                s_rden[:, ix:ix + 1], d))
                    kb.dinc("v", SV)
                    for h2 in range(2):
                        bank = PA0 if h2 == 0 else PA1
                        for qtl in range(4):
                            ix = h2 * 4 + qtl
                            src2 = bank[:, qtl * 65: qtl * 65 + 64]
                            dst2 = avn[h2][:, qtl * 64:(qtl + 1) * 64]
                            kb.op("v", lambda e, d=dst2, s_=src2, ix=ix:
                                  e.tensor_scalar(d, s_,
                                                  s_rden[:, ix:ix + 1],
                                                  None, ALU.mult))
                    kb.dinc("v", SV)
                    av_free_sv = SV.n
                    PT = PT0 if hp % 2 == 0 else PT1
                    kb.wait("t", SV)
                    kb.tmode("trans")
                    for h2 in range(2):
                        for qtl in range(4):
                            src2 = avn[h2][:, qtl * 64:(qtl + 1) * 64]
                            dst2 = PT[h2 * 64:h2 * 64 + 64,
                                      qtl * 128:(qtl + 1) * 128]
                            lastt = (h2 == 1 and qtl == 3)
                            kb.op("t", lambda e, d=dst2, s_=src2, h2=h2:
                                  e.transpose(d, s_, idb[:],
                                              tile_position=(0, h2 * 64)),
                                  inc=(ST if lastt else None))
                    kb.wait("v", ST)
                    dst3 = avfm[:, hp * 512:(hp + 1) * 512]
                    kb.op("v", lambda e, d=dst3, s_=PT: e.tensor_copy(
                        d, s_[:, 0:512]), inc=SV, drain=False)
                kb.dinc("v", SV)
                kb.wait("t", SV)
                kb.tmode("full")
                for qtl in range(4):
                    qt_g = qb * 4 + qtl
                    for ob in range(2):
                        bi, bank = next_bank()
                        for ch in range(NCH):
                            lhsT = avfm[:, ch * 512 + qtl * 128:
                                        ch * 512 + (qtl + 1) * 128]
                            rhs = wo_res[:, ch * 1024 + ob * 512:
                                         ch * 1024 + ob * 512 + 512]
                            last_mm = (ch == NCH - 1) and not with_bias_row
                            kb.op("t", lambda e, b=bank, l=lhsT, r=rhs,
                                  st=(ch == 0), sp=last_mm: e.matmul(
                                      b, l, r, start=st, stop=sp,
                                      skip_group_check=True),
                                  inc=(ST if last_mm else None))
                        if with_bias_row:
                            rhs = bias_row[:, ob * 512:(ob + 1) * 512]
                            kb.op("t", lambda e, b=bank, l=c128[:], r=rhs:
                                  e.matmul(b, l, r, start=False, stop=True,
                                           skip_group_check=True), inc=ST)
                        kb.wait("v", ST)
                        dstr = res[:, qt_g * C + ob * 512:
                                   qt_g * C + ob * 512 + 512]
                        kb.op("v", lambda e, d=dstr, p=bank: e.tensor_tensor(
                            d, d, p, ALU.add), inc=SV, drain=False)
                        free_bank(bi, SV)
                ps_sv[0] = SV.n
                ps_sv[1] = SV.n
            kb.dinc("v", SV)

        def attn_a():
            ln_and_transpose(first=(sub_idx[0] == 0))
            if debug:
                kb.wait("y", SV)
                dma_y(dbg_st[:], stats[:])
                dma_y(dbg_xtm[:], B1[:])

            def xnT_rhs(ci, qb):
                return B2[:, ci * T_LEN + qb * 512: ci * T_LEN + qb * 512 + 512]

            def xnT_lhs(ci, qt):
                return B2[:, ci * T_LEN + qt * 128: ci * T_LEN + qt * 128 + 128]

            kb.wait("v", ST)
            kb.op("v", lambda e: e.memset(vt_ones, 1.0))
            kb.dinc("v", SV)
            proj_wstat(wkT, NCH, NCH, xnT_rhs,
                       evac_to_fm(KF, 0, "v", SV), C)
            kb.dinc("v", SV)
            dma_g(B1.reshape([128, 16, 1024])[:, 0:8, :], whole_hbm(wvT),
                  guard=ST)
            wv_dw = DW.n
            kb.wait("t", DW, wv_dw)
            proj_v(xnT_lhs, NCH, B1[:, 0:8192])
            proj_wstat(wqT, NCH, NCH, xnT_rhs,
                       evac_to_fm(B1, 0, "v", SV), C)
            kb.dinc("v", SV)
            dst = bass.AP(B2, 6656, [[NT * C, 128], [1024, 8], [1, 1024]])
            dma_g(dst, whole_hbm(woT), guard=ST)
            kb.wait("t", DW)
            kb.wait("t", SV)
            kb.wait("t", SS)
            if debug:
                kb.wait("y", SV)
                kb.wait("y", SS)
                dma_y(dbg_xn[:], B2[:])
                dma_y(dbg_q[:], B1[:])
                dma_y(dbg_k[:], KF[:])
                dma_y(dbg_vt[:], VT[:])
            attention(B1, KF, with_bias_row=False)

        def ffn():
            ln_and_transpose(first=False)

            def xnT_lhs(ci, qt):
                return B2[:, ci * T_LEN + qt * 128: ci * T_LEN + qt * 128 + 128]

            if not ffn_state["q"]:
                dma_y(bass.AP(KFf, 0, [[8192, 128], [1024, 8], [1, 1024]]),
                      whole_hbm(fwT), guard=ST)
                kb.wait("v", DY)
                kb.wait("v", ST)
                w_all = KFf[:, 0:8192]
                kb.op("v", lambda e: e.tensor_reduce(
                    s_tot, w_all, AXL.X, ALU.add, apply_absolute_value=True))
                kb.dinc("v", SV)
                kb.wait("t", SV)
                kb.tmode("fp32a")
                kb.op("t", lambda e: e.matmul(
                    PA0[0:1, 0:1], onesf[:, 0:1], s_tot, start=True,
                    stop=True, skip_group_check=True), inc=ST)
                kb.wait("v", ST)
                kb.op("v", lambda e: e.tensor_copy(s_msq[0:1, 0:1],
                                                   PA0[0:1, 0:1]))
                kb.dinc("v", SV)
                kb.wait("t", SV)
                kb.tmode("fp32b")
                kb.op("t", lambda e: e.matmul(
                    PA1[:, 0:1], onesf[0:1, :], s_msq[0:1, 0:1], start=True,
                    stop=True, skip_group_check=True), inc=ST)
                kb.wait("v", ST)
                kb.op("v", lambda e: e.tensor_scalar(
                    s_scale, PA1[:, 0:1], 1.0 / (C * C), 1e-5,
                    ALU.mult, ALU.max))
                kb.dinc("v", SV)
                kb.op("v", lambda e: e.reciprocal(s_rscale, s_scale))
                kb.dinc("v", SV)
                u = VTf[:, 0:8192]
                kb.op("v", lambda e: e.tensor_scalar(
                    u, w_all, s_rscale[:, 0:1], None, ALU.mult))
                kb.dinc("v", SV)
                g_ = B1f[:, 0:8192]
                l_ = KFf[:, 0:8192]
                kb.op("v", lambda e: e.tensor_scalar(g_, u, 0.5, None,
                                                     ALU.is_gt))
                kb.op("v", lambda e: e.tensor_scalar(l_, u, -0.5, None,
                                                     ALU.is_lt))
                kb.dinc("v", SV)
                kb.op("v", lambda e: e.tensor_tensor(
                    VT8[:, 0:8192], g_, l_, ALU.subtract))
                kb.dinc("v", SV)
                kb.wait("y", SV)
                dma_y(whole_hbm(wt8),
                      bass.AP(VT8, 0, [[2 * VT_COLS, 128], [1024, 8],
                                       [1, 1024]]))
                ffn_state["q"] = True
                ffn_state["store_dy"] = DY.n
            else:
                dma_y(bass.AP(VT8, 0, [[2 * VT_COLS, 128], [1024, 8],
                                       [1, 1024]]),
                      whole_hbm(wt8), guard=ST)
            wt_dy = DY.n
            kb.wait("t", DY, wt_dy)
            for qt in range(NT):
                for ob in range(2):
                    bi, bank = next_bank()
                    for ci in range(NCH):
                        lhsT = xnT_lhs(ci, qt)
                        rhs = VT8[:, ci * 1024 + ob * 512:
                                  ci * 1024 + ob * 512 + 512]
                        kb.op("t", lambda e, b=bank, l=lhsT, r=rhs,
                              st=(ci == 0), sp=(ci == NCH - 1): e.matmul(
                                  b, l, r, start=st, stop=sp,
                                  skip_group_check=True),
                              inc=(ST if ci == NCH - 1 else None))
                    kb.wait("v", ST)
                    dstr = res[:, qt * C + ob * 512: qt * C + ob * 512 + 512]
                    kb.op("v", lambda e, d=dstr, p=bank:
                          e.scalar_tensor_tensor(d, p, s_scale[:, 0:1], d,
                                                 ALU.mult, ALU.add),
                          inc=SV, drain=False)
                    free_bank(bi, SV)
            kb.dinc("v", SV)
            if ffn_state.get("store_dy"):
                kb.wait("v", DY, ffn_state["store_dy"])
                ffn_state["store_dy"] = 0

        def attn_b():
            ln_and_transpose(first=False)

            def xnT_rhs(ci, qb):
                return B2[:, ci * T_LEN + qb * 512: ci * T_LEN + qb * 512 + 512]

            kb.wait("v", ST)
            kb.op("v", lambda e: e.memset(vt_ones, 1.0))
            kb.dinc("v", SV)
            proj_wstat(akd, NCH, 4, xnT_rhs,
                       evac_to_fm(B1, 0, "s", SS, bias_col=bkd_t,
                                  func=AF.Gelu), C)
            kb.dinc("s", SS)
            kb.wait("v", SS)
            for lc in range(2):
                dst = B1[:, (4 + lc) * T_LEN:(5 + lc) * T_LEN]
                a_ = B1[:, lc * T_LEN:(lc + 1) * T_LEN]
                b_ = B1[:, (2 + lc) * T_LEN:(3 + lc) * T_LEN]
                kb.op("v", lambda e, d=dst, a=a_, b=b_: e.tensor_tensor(
                    d, a, b, ALU.add))
            kb.dinc("v", SV)

            def lat_rhs(ci, qb):
                base = (4 + ci) * T_LEN + qb * 512
                return B1[:, base:base + 512]

            def lat_lhs(ci, qt):
                base = (4 + ci) * T_LEN + qt * 128
                return B1[:, base:base + 128]

            kb.wait("t", SV)
            proj_wstat(aku, 2, NCH, lat_rhs,
                       evac_to_fm(KF, 0, "v", SV, bias_col=bkuk_t,
                                  func=AF.Identity), LAT, base_col=0)
            kb.dinc("v", SV)
            dma_g(B1.reshape([128, 16, 1024])[:, 0:2, :],
                  aku[:, C:2 * C].rearrange("(ci p) co -> p ci co", p=128),
                  guard=SV)
            kb.wait("t", DW)
            kb.wait("t", SS)
            proj_v(lat_lhs, 2, B1[:, 0:2048])
            proj_wstat(aq, NCH, NCH, xnT_rhs,
                       evac_to_fm(B1, 0, "v", SV, bias_col=bq_t,
                                  func=AF.Identity), C)
            kb.dinc("v", SV)
            dst = bass.AP(B2, 6656, [[NT * C, 128], [1024, 8], [1, 1024]])
            dma_g(dst, whole_hbm(awo), guard=ST)
            kb.wait("t", DW)
            kb.tmode("full")
            for ob in range(2):
                bank = PA0[:, 0:512] if ob == 0 else PA1[:, 0:512]
                for ci in range(NCH):
                    lhsT = foldt[:, ci * 128:(ci + 1) * 128]
                    rhs = wo_res[:, ci * 1024 + ob * 512:
                                 ci * 1024 + ob * 512 + 512]
                    kb.op("t", lambda e, b=bank, l=lhsT, r=rhs, st=(ci == 0),
                          sp=(ci == NCH - 1): e.matmul(
                              b, l, r, start=st, stop=sp,
                              skip_group_check=True),
                          inc=(ST if ci == NCH - 1 else None))
                kb.wait("v", ST)
                dst2 = bias_row[:, ob * 512:(ob + 1) * 512]
                kb.op("v", lambda e, d=dst2, p=bank, ob=ob: e.tensor_tensor(
                    d, p, bo_bf[:, ob * 512:(ob + 1) * 512], ALU.add))
            kb.dinc("v", SV)
            kb.wait("t", SV)
            kb.wait("t", SS)
            attention(B1, KF, with_bias_row=True)

        ffn_state = {"q": False, "store_dy": 0}
        sub_idx = [0]
        phases = [attn_a, ffn, attn_b, ffn]
        for seq in range(2):
            if sub_idx[0] >= n_sublayers:
                break
            if seq == 1:
                kb.wait("y", SV)
                DO.n += 16
                kb.emit("y", lambda e: e.dma_start(
                    y0.rearrange("(t p) c -> p t c", p=128),
                    res3).then_inc(DO.h, 16))
                kb.wait("y", DO)
                dma_y(res3,
                      x1.rearrange("(t p) c -> p t c", p=128))
                xdy = DY.n
                kb.wait("v", DY, xdy)
                kb.wait("s", DY, xdy)
            for ph in phases:
                if sub_idx[0] >= n_sublayers:
                    break
                ph()
                sub_idx[0] += 1
        ran_seq1 = sub_idx[0] > 4
        kb.wait("y", SV)
        if ran_seq1:
            DO.n += 16
            kb.emit("y", lambda e: e.dma_start(
                y1.rearrange("(t p) c -> p t c", p=128),
                res3).then_inc(DO.h, 16))
        else:
            DO.n += 16
            kb.emit("y", lambda e: e.dma_start(
                y0.rearrange("(t p) c -> p t c", p=128),
                res3).then_inc(DO.h, 16))
            DO.n += 16
            kb.emit("y", lambda e: e.dma_start(
                y1.rearrange("(t p) c -> p t c", p=128),
                res3).then_inc(DO.h, 16))
        kb.wait("y", DO)

        with nc.Block() as block:
            @block.tensor
            def _(t):
                for f in kb.ops["t"]:
                    f(t)

            @block.vector
            def _(v):
                for f in kb.ops["v"]:
                    f(v)

            @block.scalar
            def _(s):
                for f in kb.ops["s"]:
                    f(s)

            @block.gpsimd
            def _(g):
                for f in kb.ops["g"]:
                    f(g)

            @block.sync
            def _(y):
                for f in kb.ops["y"]:
                    f(y)

    return nc


def _shared_inputs(inputs):
    d = inputs

    def a(x):
        return np.ascontiguousarray(np.asarray(x, dtype=np.float32))

    return {
        "wqT": a(np.asarray(d["L0_attn_a_wq"]).T),
        "wkT": a(np.asarray(d["L0_attn_a_wk"]).T),
        "wvT": a(np.asarray(d["L0_attn_a_wv"]).T),
        "woT": a(np.asarray(d["L0_attn_a_wo"]).T),
        "aq": a(d["L0_attn_b_Wq"]),
        "akd": a(d["L0_attn_b_Wkd"]),
        "aku": a(d["L0_attn_b_Wku"]),
        "awo": a(d["L0_attn_b_Wo"]),
        "fwT": a(np.asarray(d["L0_ffn_weight"]).T),
        "bq": a(np.asarray(d["L0_attn_b_bq"]).reshape(8, 128).T),
        "bkd": a(np.asarray(d["L0_attn_b_bkd"]).reshape(4, 128).T),
        "bkuk": a(np.asarray(d["L0_attn_b_bku"])[:1024].reshape(8, 128).T),
        "bkuv": a(np.broadcast_to(
            np.asarray(d["L0_attn_b_bku"])[1024:].reshape(8, 128).T[:, :, None],
            (128, 8, 128)).reshape(128, 1024)),
        "bo": a(np.broadcast_to(
            np.asarray(d["L0_attn_b_bo"])[None, :], (128, 1024))),
    }


def kernel(**inputs):
    nc = build_program()
    sh = _shared_inputs(inputs)
    x = np.asarray(inputs["L0_attn_a_x"], dtype=np.float32)
    in_maps = []
    for core in range(8):
        m = dict(sh)
        m["x0"] = np.ascontiguousarray(x[2 * core])
        m["x1"] = np.ascontiguousarray(x[2 * core + 1])
        in_maps.append(m)
    res = run_bass_kernel_spmd(nc, in_maps, core_ids=list(range(8)))
    out = np.empty((16, T_LEN, C), dtype=np.float32)
    for core in range(8):
        out[2 * core] = res.results[core]["y0"]
        out[2 * core + 1] = res.results[core]["y1"]
    return out
